# version 22
# baseline (speedup 1.0000x reference)
import numpy as np
from contextlib import ExitStack

import concourse.bass as bass
import concourse.mybir as mybir
from concourse.bass_utils import run_bass_kernel_spmd

F32 = mybir.dt.float32
BF16 = mybir.dt.bfloat16
FP8 = mybir.dt.float8e4
AX = mybir.AxisListType.X
ALU = mybir.AluOpType
ACT = mybir.ActivationFunctionType

P = 128
D = 2048
KD = D // P
TC = 256
JT = TC // P
TOK = 4096
NRC = TOK // TC
W = 256
NCH = NRC + 1
MAGIC = 12582912.0
EPS = 1e-5
N_CORES = 8


class Ctr:
    def __init__(self):
        self.d = 0
        self.g = 0
        self.v = 0
        self.s = 0
        self.t = 0


def build():
    nc = bass.Bass("TRN2", target_bir_lowering=False, debug=False)

    xw = nc.dram_tensor("xw", [W + TOK, D], F32, kind="ExternalInput").ap()
    wi = nc.dram_tensor("wi", [P, KD * D], F32, kind="ExternalInput").ap()
    wf = nc.dram_tensor("wf", [P, KD * D], F32, kind="ExternalInput").ap()
    wg = nc.dram_tensor("wg", [P, KD * D], F32, kind="ExternalInput").ap()
    wot = nc.dram_tensor("wot", [P, KD * D], F32, kind="ExternalInput").ap()
    wft = nc.dram_tensor("wft", [P, KD * D], F32, kind="ExternalInput").ap()
    s_i = nc.dram_tensor("s_i", [P, KD], F32, kind="ExternalInput").ap()
    s_f = nc.dram_tensor("s_f", [P, KD], F32, kind="ExternalInput").ap()
    s_g = nc.dram_tensor("s_g", [P, KD], F32, kind="ExternalInput").ap()
    s_o = nc.dram_tensor("s_o", [P, KD], F32, kind="ExternalInput").ap()
    gnw = nc.dram_tensor("gnw", [P, KD], F32, kind="ExternalInput").ap()
    bias_bc = nc.dram_tensor("bias_bc", [P, D], F32,
                             kind="ExternalInput").ap()
    ws_bc = nc.dram_tensor("ws_bc", [P, 1], F32, kind="ExternalInput").ap()
    idm = nc.dram_tensor("idm", [P, P], F32, kind="ExternalInput").ap()

    out = nc.dram_tensor("out", [TOK, D], F32, kind="ExternalOutput").ap()
    out_v = out.rearrange("(c j p) d -> c j p d", j=JT, p=P)
    xw_v = xw.rearrange("(c j p) d -> c j p d", j=JT, p=P)

    xqt_sp = nc.dram_tensor("xqt_sp", [NRC, P, KD * TC], BF16).ap()
    hn_sp = nc.dram_tensor("hn_sp", [NRC, P, KD * TC], BF16).ap()
    x1_sp = nc.dram_tensor("x1_sp", [NRC, P, JT * D], F32).ap()
    cc_in = nc.dram_tensor("cc_in", [P, 1], F32)
    cc_out = nc.dram_tensor("cc_out", [P, 1], F32)

    n = Ctr()

    with ExitStack() as st:
        e = st.enter_context
        w8a = e(nc.sbuf_tensor("w8a", [P, KD * D], FP8))
        w8b = e(nc.sbuf_tensor("w8b", [P, KD * D], FP8))
        x_bufs = [e(nc.sbuf_tensor(f"x{i}", [P, JT * D], F32))
                  for i in range(2)]
        h_bufs = [e(nc.sbuf_tensor(f"h{i}", [P, KD * TC], F32))
                  for i in range(2)]
        xn = e(nc.sbuf_tensor("xn", [P, D], F32))
        xq_tm = e(nc.sbuf_tensor("xq_tm", [P, JT * D], BF16))
        xqt_b = [e(nc.sbuf_tensor(f"xqt{i}", [P, KD * TC], BF16))
                 for i in range(2)]
        f_b = e(nc.sbuf_tensor("f_b", [P, KD * TC], BF16))
        fm1 = e(nc.sbuf_tensor("fm1", [P, KD * TC], BF16))
        ih = e(nc.sbuf_tensor("ih", [P, KD * TC], BF16))
        state = e(nc.sbuf_tensor("state", [P, KD], F32))
        hn = e(nc.sbuf_tensor("hn", [P, KD * TC], BF16))
        rmsbc = e(nc.sbuf_tensor("rmsbc", [P, TC], F32))
        e1 = xn[:, 512:1024]
        zq = hn.bitcast(FP8)[:, 0:JT * D]
        zqt = hn.bitcast(FP8)[:, JT * D:2 * JT * D]
        stats_m = e(nc.sbuf_tensor("stats_m", [P, JT * NRC], F32))
        stats_r = e(nc.sbuf_tensor("stats_r", [P, JT * NRC], F32))
        stats_r2 = e(nc.sbuf_tensor("stats_r2", [P, JT * NRC], F32))
        stats_b = e(nc.sbuf_tensor("stats_b", [P, JT * NRC], F32))
        srow = e(nc.sbuf_tensor("srow", [1, TC], F32))
        rrow = e(nc.sbuf_tensor("rrow", [1, TC], F32))
        rcol = e(nc.sbuf_tensor("rcol", [P, JT], F32))
        mcol = e(nc.sbuf_tensor("mcol", [P, JT], F32))
        rrcol = e(nc.sbuf_tensor("rrcol", [P, JT], F32))
        tcol = e(nc.sbuf_tensor("tcol", [P, 4 * JT], F32))
        sumc = e(nc.sbuf_tensor("sumc", [P, JT], F32))
        sqc = e(nc.sbuf_tensor("sqc", [P, JT], F32))
        runmax = e(nc.sbuf_tensor("runmax", [P, 1], F32))
        rmax2 = e(nc.sbuf_tensor("rmax2", [P, 1], F32))
        rowg = e(nc.sbuf_tensor("rowg", [1, P], F32))
        grow = e(nc.sbuf_tensor("grow", [1, 4], F32))
        invh = e(nc.sbuf_tensor("invh", [P, 1], F32))
        mgc = e(nc.sbuf_tensor("mgc", [P, 1], F32))
        epsc = e(nc.sbuf_tensor("epsc", [P, 1], F32))
        onescol = e(nc.sbuf_tensor("onescol", [P, 1], BF16))
        onesrow = e(nc.sbuf_tensor("onesrow", [1, P], F32))
        idm_b = e(nc.sbuf_tensor("idm_b", [P, P], BF16))
        si_s = e(nc.sbuf_tensor("si_s", [P, KD], F32))
        sf_s = e(nc.sbuf_tensor("sf_s", [P, KD], F32))
        sg_s = e(nc.sbuf_tensor("sg_s", [P, KD], F32))
        so_s = e(nc.sbuf_tensor("so_s", [P, KD], F32))
        gnw_s = e(nc.sbuf_tensor("gnw_s", [P, KD], F32))
        nsf_s = e(nc.sbuf_tensor("nsf_s", [P, KD], F32))
        gso_s = e(nc.sbuf_tensor("gso_s", [P, KD], F32))
        wsb_s = e(nc.sbuf_tensor("wsb_s", [P, 1], F32))
        ws2_s = e(nc.sbuf_tensor("ws2_s", [P, 1], F32))
        bias_s = e(nc.sbuf_tensor("bias_s", [P, D], F32))
        sqdump = bias_s
        ptrA = e(nc.psum_tensor("ptrA", [P, 512], F32))
        ptrB = e(nc.psum_tensor("ptrB", [P, 512], F32))
        pfA = e(nc.psum_tensor("pfA", [P, TC], F32))
        pfB = e(nc.psum_tensor("pfB", [P, TC], F32))
        piA = e(nc.psum_tensor("piA", [P, 512], F32))
        piB = e(nc.psum_tensor("piB", [P, 512], F32))
        prow = e(nc.psum_tensor("prow", [1, TC], F32))
        pbc = e(nc.psum_tensor("pbc", [P, TC], F32))
        dsc = e(nc.semaphore("dsc"))
        dse = e(nc.semaphore("dse"))
        dso = e(nc.semaphore("dso"))
        gw = e(nc.semaphore("gw"))
        gxq = e(nc.semaphore("gxq"))
        ghn = e(nc.semaphore("ghn"))
        gx1 = e(nc.semaphore("gx1"))
        gout = e(nc.semaphore("gout"))
        gcc = e(nc.semaphore("gcc"))
        scc = e(nc.semaphore("scc"))
        gw2 = e(nc.semaphore("gw2"))
        vs = e(nc.semaphore("vs"))
        ss = e(nc.semaphore("ss"))
        pe = e(nc.semaphore("pe"))
        cs = e(nc.semaphore("cs"))
        block = e(nc.Block())

        dn = {"dsc": 0, "dse": 0, "dso": 0, "gw": 0, "gxq": 0,
              "ghn": 0, "gx1": 0, "gout": 0, "gcc": 0, "scc": 0,
              "gw2": 0}
        dsem_of = {"dsc": dsc, "dse": dse, "dso": dso, "gw": gw,
                   "gxq": gxq, "ghn": ghn, "gx1": gx1, "gout": gout,
                   "gcc": gcc, "scc": scc, "gw2": gw2}
        dxp = ["dse", "dso"]

        def dma(eng, name, out_ap, in_ap, fence=True, **kw):
            if fence and dn[name] > 0:
                eng.wait_ge(dsem_of[name], 16 * dn[name])
            eng.dma_start(out=out_ap, in_=in_ap, **kw).then_inc(
                dsem_of[name], 16)
            dn[name] += 1
            return dn[name]

        def dwait(eng, name, cnt=None):
            eng.wait_ge(dsem_of[name], 16 * (dn[name] if cnt is None
                                             else cnt))

        def V(inst):
            inst.then_inc(vs, 1)
            n.v += 1
            return n.v

        ptr = [ptrA, ptrB]
        pf = [pfA, pfB]
        pi = [piA, piB]

        @block.sync
        def _(sync):
            dma(sync, "dsc", si_s[:], s_i)
            dma(sync, "dsc", sf_s[:], s_f, fence=False)
            dma(sync, "dsc", sg_s[:], s_g, fence=False)
            dma(sync, "dsc", so_s[:], s_o, fence=False)
            dma(sync, "dsc", gnw_s[:], gnw, fence=False)
            dma(sync, "dsc", wsb_s[:], ws_bc, fence=False)
            for j in range(JT):
                dma(sync, "dse",
                    x_bufs[0][:, j * D:(j + 1) * D], xw_v[0, j],
                    fence=(j == 0))
        x0_tgt = ("dse", dn["dse"])

        G_WF = [0] * 4
        G_WI = [0] * 4

        @block.gpsimd
        def _(gpsimd):
            dma(gpsimd, "gw", idm_b[:], idm)
        G_IDM = dn["gw"]

        @block.gpsimd
        def _(gpsimd):
            for g4 in range(4):
                for mm4 in range(4):
                    m = g4 * 4 + mm4
                    dma(gpsimd, "gw", w8b[:, m * D:(m + 1) * D],
                        wf[:, m * D:(m + 1) * D], fence=(mm4 == 0))
                G_WF[g4] = dn["gw"]
            for g4 in range(4):
                for mm4 in range(4):
                    m = g4 * 4 + mm4
                    dma(gpsimd, "gw", w8a[:, m * D:(m + 1) * D],
                        wi[:, m * D:(m + 1) * D], fence=(mm4 == 0))
                G_WI[g4] = dn["gw"]
        G_SETUP = dn["gw"]

        @block.vector
        def _(vector):
            V(vector.memset(mgc[:], MAGIC))
            V(vector.memset(epsc[:], EPS))
            V(vector.memset(onescol[:], 1.0))
            V(vector.memset(onesrow[:], 1.0))
            V(vector.memset(runmax[:], 0.0))
            V(vector.memset(state[:], 0.0))
            dwait(vector, "dsc")
            V(vector.tensor_scalar(nsf_s[:], sf_s[:], -1.0, None, ALU.mult))
            V(vector.tensor_tensor(gso_s[:], gnw_s[:], so_s[:], ALU.mult))
            V(vector.tensor_scalar(ws2_s[:], wsb_s[:], 2.0, None, ALU.mult))
        V_SETUP = n.v

        p1 = {}

        def p1_norm_pre(ci):
            rec = p1.setdefault(ci, {})
            if ci + 1 < NCH:
                @block.sync
                def _(sync):
                    if ci >= 1:
                        sync.wait_ge(vs, p1[ci - 1]["v_norm"])
                    for j in range(JT):
                        dma(sync, dxp[(ci + 1) % 2],
                            x_bufs[(ci + 1) % 2][:, j * D:(j + 1) * D],
                            xw_v[ci + 1, j], fence=(j == 0))
                rec["d_x_next"] = (dxp[(ci + 1) % 2],
                                   dn[dxp[(ci + 1) % 2]])
            else:
                rec["d_x_next"] = None

            xb = x_bufs[ci % 2]
            d_x_ready = p1[ci - 1]["d_x_next"] if ci >= 1 else x0_tgt

            @block.scalar
            def _(scalar):
                if ci == 0:
                    scalar.wait_ge(vs, V_SETUP)
                scalar.wait_ge(dsem_of[d_x_ready[0]], 16 * d_x_ready[1])
                for j in range(JT):
                    if j == 1:
                        scalar.wait_ge(ss, n.s + 1)
                    scalar.activation(
                        sqdump[:],
                        xb[:, j * D:(j + 1) * D], ACT.Square,
                        accum_out=sqc[:, j:j + 1],
                    ).then_inc(ss, 1)
            n.s += JT
            rec["s_sq"] = n.s

            @block.vector
            def _(vector):
                vector.wait_ge(vs, n.v)
                vector.wait_ge(dsem_of[d_x_ready[0]], 16 * d_x_ready[1])
                for j in range(JT):
                    V(vector.reduce_sum(sumc[:, j:j + 1],
                                        xb[:, j * D:(j + 1) * D], AX))
                vector.wait_ge(ss, rec["s_sq"])
                vector.wait_ge(vs, n.v)
                V(vector.tensor_scalar(mcol[:, 0:JT], sumc[:, 0:JT],
                                       1.0 / D, None, ALU.mult))
                V(vector.tensor_scalar(tcol[:, JT:2 * JT], sqc[:, 0:JT],
                                       1.0 / D, None, ALU.mult))
                vector.wait_ge(vs, n.v)
                V(vector.tensor_tensor(tcol[:, 0:JT], mcol[:, 0:JT],
                                       mcol[:, 0:JT], ALU.mult))
                vector.wait_ge(vs, n.v)
                V(vector.tensor_tensor(tcol[:, 2 * JT:3 * JT],
                                       tcol[:, JT:2 * JT],
                                       tcol[:, 0:JT], ALU.subtract))
            rec["v_var"] = n.v

        def p1_norm_sqrt(ci):
            rec = p1[ci]

            @block.scalar
            def _(scalar):
                scalar.wait_ge(vs, rec["v_var"])
                scalar.activation(tcol[:, 3 * JT:4 * JT],
                                  tcol[:, 2 * JT:3 * JT], ACT.Sqrt,
                                  bias=epsc[:, 0:1]).then_inc(ss, 1)
            n.s += 1
            rec["s_std"] = n.s

        def p1_norm_fin_ops(ci):
            rec = p1[ci]
            xb = x_bufs[ci % 2]

            def op0(vector):
                vector.wait_ge(ss, rec["s_std"])
                V(vector.reciprocal(rcol[:, 0:JT], tcol[:, 3 * JT:4 * JT]))

            def op1(vector):
                vector.wait_ge(vs, n.v)
                V(vector.tensor_scalar(rrcol[:, 0:JT], rcol[:, 0:JT],
                                       256.0, None, ALU.mult))

            def op2(vector):
                vector.wait_ge(vs, n.v)
                V(vector.tensor_tensor(tcol[:, 0:JT], mcol[:, 0:JT],
                                       rrcol[:, 0:JT], ALU.mult))
                vector.wait_ge(vs, n.v)
                V(vector.tensor_scalar(tcol[:, JT:2 * JT], tcol[:, 0:JT],
                                       -1.0, mgc[:, 0:1],
                                       ALU.mult, ALU.add))

            def mk_big(j, second):
                def f(vector):
                    vector.wait_ge(vs, n.v)
                    if not second:
                        if j == 0 and ci >= 1:
                            vector.wait_ge(pe, p1[ci - 1]["t_tr"])
                        V(vector.tensor_scalar(
                            xn[:], xb[:, j * D:(j + 1) * D],
                            rrcol[:, j:j + 1], tcol[:, JT + j:JT + j + 1],
                            ALU.mult, ALU.add))
                    else:
                        V(vector.tensor_scalar(
                            xq_tm[:, j * D:(j + 1) * D], xn[:],
                            mgc[:, 0:1], 1.0 / 256.0,
                            ALU.subtract, ALU.mult))
                        if j == JT - 1:
                            rec["v_norm"] = n.v
                return f

            return [op0, op1, op2, mk_big(0, False), mk_big(0, True),
                    mk_big(1, False), mk_big(1, True)]

        def p1_norm_full(ci):
            p1_norm_pre(ci)
            p1_norm_sqrt(ci)
            ops = p1_norm_fin_ops(ci)

            @block.vector
            def _(vector):
                for f in ops:
                    f(vector)

        tr4 = [ptrA, ptrB, piA, piB]

        def p1_tr(ci, mid_hook=None, pi_rec=None):
            rec = p1.setdefault(ci, {})
            if pi_rec is None and ci >= 2:
                pi_rec = p1[ci - 2]
            for p in range(8):
                @block.tensor
                def _(tensor):
                    if p == 0:
                        if ci == 0:
                            tensor.wait_ge(gw, 16 * G_IDM)
                        tensor.wait_ge(vs, rec["v_norm"])
                    if p in (2, 3) and pi_rec is not None:
                        tensor.wait_ge(vs, pi_rec[f"v_i{12 + p}"])
                    if p >= 4:
                        tensor.wait_ge(vs, rec[f"v_tr{p - 4}"][1])
                    bank = tr4[p % 4]
                    for kk in range(2):
                        k = 2 * p + kk
                        for j in range(JT):
                            mm = tensor.matmul(
                                bank[:, kk * TC + j * P:
                                     kk * TC + (j + 1) * P],
                                xq_tm[:, j * D + k * P:j * D + (k + 1) * P],
                                idm_b[:], start=True, stop=True)
                    mm.then_inc(pe, 1)
                n.t += 1
                rec[f"t_tr{p}"] = n.t

                @block.vector
                def _(vector):
                    vector.wait_ge(pe, rec[f"t_tr{p}"])
                    if p == 0 and ci >= 2:
                        vector.wait_ge(pe, p1[ci - 2]["t_i_last"])
                        if "g_xqt" in p1[ci - 2]:
                            vector.wait_ge(gxq,
                                           16 * p1[ci - 2]["g_xqt"])
                    V(vector.tensor_copy(
                        xqt_b[ci % 2][:, 2 * p * TC:(2 * p + 2) * TC],
                        tr4[p % 4][:]))
                rec[f"v_tr{p}"] = ("v", n.v)
                if p == 1 and mid_hook is not None:
                    mid_hook()
            rec["t_tr"] = rec["t_tr7"]
            rec["v_tr"] = rec["v_tr7"]

            if ci >= 1:
                @block.gpsimd
                def _(gpsimd):
                    gpsimd.wait_ge(vs, rec["v_tr"][1])
                    dma(gpsimd, "gxq", xqt_sp[ci - 1], xqt_b[ci % 2][:])
                rec["g_xqt"] = dn["gxq"]

        def p1_head_b(ci, hn_ops=None, norm_chunk=None, tr_sel=None):
            rec = p1[ci]
            hb = h_bufs[ci % 2]
            xqt = xqt_b[ci % 2]
            tr_rec = p1[tr_sel if tr_sel is not None
                        else min(ci + 1, NCH - 1)]
            norm_ops = (p1_norm_fin_ops(norm_chunk)
                        if norm_chunk is not None else None)
            for m in range(KD):
                @block.tensor
                def _(tensor):
                    if m == 0:
                        tensor.wait_ge(vs, rec["v_tr"][1])
                    if ci == 0 and m % 4 == 0:
                        tensor.wait_ge(gw, 16 * G_WF[m // 4])
                    if m >= 2:
                        tensor.wait_ge(ss, rec[f"s_f{m - 2}"])
                    elif ci >= 1:
                        tensor.wait_ge(ss, p1[ci - 1][f"s_f{m + 14}"])
                    for k in range(KD):
                        mm = tensor.matmul(
                            pf[m % 2][:],
                            w8b[:, m * D + k * P:m * D + (k + 1) * P],
                            xqt[:, k * TC:(k + 1) * TC],
                            start=(k == 0), stop=(k == KD - 1))
                    mm.then_inc(pe, 1)
                n.t += 1
                rec[f"t_f{m}"] = n.t

                @block.scalar
                def _(scalar):
                    scalar.wait_ge(pe, rec[f"t_f{m}"])
                    if m == 0 and ci >= 1:
                        scalar.wait_ge(vs, p1[ci - 1]["v_scan"])
                    scalar.activation(f_b[:, m * TC:(m + 1) * TC],
                                      pf[m % 2][:], ACT.Sigmoid,
                                      scale=sf_s[:, m:m + 1]).then_inc(ss, 1)
                    scalar.activation(fm1[:, m * TC:(m + 1) * TC],
                                      pf[m % 2][:], ACT.Sigmoid,
                                      scale=nsf_s[:, m:m + 1]).then_inc(ss, 1)
                n.s += 2
                rec[f"s_f{m}"] = n.s

                @block.tensor
                def _(tensor):
                    if ci == 0 and m % 4 == 0:
                        tensor.wait_ge(gw, 16 * G_WI[m // 4])
                    if m >= 2:
                        tensor.wait_ge(vs, rec[f"v_i{m - 2}"])
                    elif ci >= 1:
                        tensor.wait_ge(vs, p1[ci - 1][f"v_i{m + 14}"])
                    if m < 2:
                        tensor.wait_ge(vs, tr_rec[f"v_tr{6 + m}"][1])
                    for k in range(KD):
                        mm = tensor.matmul(
                            pi[m % 2][:, 0:TC],
                            w8a[:, m * D + k * P:m * D + (k + 1) * P],
                            xqt[:, k * TC:(k + 1) * TC],
                            start=(k == 0), stop=(k == KD - 1))
                    mm.then_inc(pe, 1)
                n.t += 1
                rec[f"t_i{m}"] = n.t

                @block.vector
                def _(vector):
                    vector.wait_ge(pe, rec[f"t_i{m}"])
                    vector.wait_ge(ss, rec[f"s_f{m}"])
                    if m == 0 and ci >= 2:
                        vector.wait_ge(pe, p1[ci - 1]["t_rsum"])
                    if m == 0 and ci >= 1:
                        vector.wait_ge(vs, p1[ci - 1]["v_save"])
                    V(vector.scalar_tensor_tensor(
                        ih[:, m * TC:(m + 1) * TC], pi[m % 2][:, 0:TC],
                        si_s[:, m:m + 1], fm1[:, m * TC:(m + 1) * TC],
                        ALU.mult, ALU.mult))
                    vector.wait_ge(vs, n.v)
                    V(vector.tensor_tensor_scan(
                        hb[:, m * TC:(m + 1) * TC],
                        f_b[:, m * TC:(m + 1) * TC],
                        ih[:, m * TC:(m + 1) * TC],
                        state[:, m:m + 1], ALU.mult, ALU.add))
                    if ci >= 1:
                        vector.wait_ge(vs, n.v)
                        V(vector.tensor_tensor(
                            ih[:, m * TC:(m + 1) * TC],
                            hb[:, m * TC:(m + 1) * TC],
                            hb[:, m * TC:(m + 1) * TC], ALU.mult))
                    rec[f"v_h2k{m}"] = n.v
                    rec[f"v_i{m}"] = n.v
                    if hn_ops is not None and m < 8:
                        hn_ops[m](vector)
                    if norm_ops is not None and 8 <= m <= 14:
                        norm_ops[m - 8](vector)
                if norm_chunk is not None and m == 6:
                    p1_norm_sqrt(norm_chunk)
            rec["t_i_last"] = rec[f"t_i{KD - 1}"]
            rec["v_scan"] = n.v
            rec["s_h2"] = ("v", n.v)

            @block.vector
            def _(vector):
                vector.wait_ge(vs, n.v)
                V(vector.tensor_copy(state[:], hb[:, TC - 1::TC]))
            rec["v_save"] = n.v

        def p1_tail_pe(ci):
            rec = p1[ci]

            @block.tensor
            def _(tensor):
                if ci >= 2:
                    tensor.wait_ge(ss, p1[ci - 1]["s_rms"])
                for k in range(KD):
                    tensor.wait_ge(vs, rec[f"v_h2k{k}"])
                    mm = tensor.matmul(prow[:], onescol[:],
                                       ih[:, k * TC:(k + 1) * TC],
                                       start=(k == 0), stop=(k == KD - 1))
                mm.then_inc(pe, 1)
            n.t += 1
            rec["t_rsum"] = n.t

        def p1_tail_s(ci):
            rec = p1[ci]

            @block.scalar
            def _(scalar):
                scalar.wait_ge(pe, rec["t_rsum"])
                scalar.activation(srow[:], prow[:], ACT.Sqrt,
                                  bias=epsc[0:1, 0:1], scale=1.0 / D
                                  ).then_inc(ss, 1)
            n.s += 1
            rec["s_rms"] = n.s

        def p1_tail_v(ci):
            rec = p1[ci]

            @block.vector
            def _(vector):
                vector.wait_ge(ss, rec["s_rms"])
                if ci >= 2:
                    vector.wait_ge(pe, p1[ci - 1]["t_bc"])
                V(vector.reciprocal(rrow[:], srow[:]))
            rec["v_rr"] = n.v

        def p1_bc_pe(ci):
            rec = p1[ci]

            @block.tensor
            def _(tensor):
                tensor.wait_ge(vs, rec["v_rr"])
                if ci >= 2:
                    tensor.wait_ge(vs, p1[ci - 1]["v_bc"])
                tensor.matmul(pbc[:], onesrow[:], rrow[:],
                              start=True, stop=True).then_inc(pe, 1)
            n.t += 1
            rec["t_bc"] = n.t

        def p1_bc_v(ci):
            rec = p1[ci]

            @block.vector
            def _(vector):
                vector.wait_ge(pe, rec["t_bc"])
                V(vector.tensor_copy(rmsbc[:], pbc[:]))
            rec["v_bc"] = n.v

        def p1_hn_ops(ci):
            rec = p1[ci]
            hb = h_bufs[ci % 2]

            def mk(q):
                def f(vector):
                    if q == 0:
                        if ci >= 2:
                            vector.wait_ge(ghn, 16 * p1[ci - 1]["g_hn"])
                        vector.wait_ge(vs, n.v)
                    for k in (2 * q, 2 * q + 1):
                        V(vector.scalar_tensor_tensor(
                            hn[:, k * TC:(k + 1) * TC],
                            hb[:, k * TC:(k + 1) * TC],
                            gso_s[:, k:k + 1], rmsbc[:],
                            ALU.mult, ALU.mult))
                    if q == 7:
                        rec["v_hn"] = n.v
                return f
            return [mk(q) for q in range(8)]

        def p1_hn_sp(ci):
            rec = p1[ci]

            @block.gpsimd
            def _(gpsimd):
                gpsimd.wait_ge(vs, rec["v_hn"])
                dma(gpsimd, "ghn", hn_sp[ci - 1], hn[:])
            rec["g_hn"] = dn["ghn"]

        p1_norm_full(0)
        for ci in range(NCH):
            if ci >= 3:
                p1_tail_pe(ci - 2)
                p1_tail_s(ci - 2)
                p1_tr(ci, mid_hook=lambda c=ci: p1_tail_v(c - 2))
            elif ci == 1:
                p1_head_b(0, tr_sel=0)
                p1_tr(1, pi_rec=p1[0])
                p1_norm_full(2)
            else:
                p1_tr(ci)
            if ci >= 3:
                p1_bc_pe(ci - 2)
                p1_bc_v(ci - 2)
            if ci == 0:
                p1_norm_full(1)
            elif ci >= 2 and ci + 1 < NCH:
                p1_norm_pre(ci + 1)
            if ci == 1:
                continue
            if ci >= 1:
                p1_head_b(
                    ci - 1,
                    hn_ops=p1_hn_ops(ci - 2) if ci >= 3 else None,
                    norm_chunk=ci + 1 if 2 <= ci + 1 < NCH else None)
            if ci >= 3:
                p1_hn_sp(ci - 2)

        p1_tail_pe(NCH - 2)
        p1_tail_s(NCH - 2)
        p1_tail_v(NCH - 2)
        p1_bc_pe(NCH - 2)
        p1_bc_v(NCH - 2)
        p1_head_b(NCH - 1, hn_ops=p1_hn_ops(NCH - 2))
        p1_hn_sp(NCH - 2)

        G_WG = [0] * 4
        G_WOT = [0] * 4
        lastp1 = p1[NCH - 1]

        @block.gpsimd
        def _(gpsimd):
            for g4 in range(4):
                gpsimd.wait_ge(pe, lastp1[f"t_f{4 * g4 + 3}"])
                dma(gpsimd, "gw2", w8b[:, g4 * 8192:(g4 + 1) * 8192],
                    wot[:, g4 * 8192:(g4 + 1) * 8192])
                G_WOT[g4] = dn["gw2"]
                gpsimd.wait_ge(pe, lastp1[f"t_i{4 * g4 + 3}"])
                for mm4 in range(4):
                    m = 4 * g4 + mm4
                    dma(gpsimd, "gw", w8a[:, m * D:(m + 1) * D],
                        wg[:, m * D:(m + 1) * D], fence=(mm4 == 0))
                G_WG[g4] = dn["gw"]

        p1_tail_pe(NCH - 1)
        p1_tail_s(NCH - 1)
        p1_tail_v(NCH - 1)
        p1_bc_pe(NCH - 1)
        p1_bc_v(NCH - 1)
        last_hn = p1_hn_ops(NCH - 1)

        @block.vector
        def _(vector):
            for f in last_hn:
                f(vector)
        p1_hn_sp(NCH - 1)

        p2 = {}
        xqt_bufs = xqt_b
        hn_bufs = [hn, ih]
        last1 = p1[NCH - 1]

        def p2_load(c):
            rec = {}
            p2[c] = rec
            xqb = xqt_bufs[c % 2]
            hnb = hn_bufs[c % 2]
            xb = x_bufs[c % 2]

            @block.sync
            def _(sync):
                par = dxp[c % 2]
                if c >= 2:
                    sync.wait_ge(pe, p2[c - 2]["t_g"])
                else:
                    sync.wait_ge(pe, last1["t_i_last"])
                    dwait(sync, "gxq")
                dma(sync, par, xqb[:], xqt_sp[c])
                rec["d_ld"] = (par, dn[par])
                if c >= 2:
                    sync.wait_ge(vs, p2[c - 2]["v_x1"])
                for j in range(JT):
                    dma(sync, par, xb[:, j * D:(j + 1) * D],
                        xw_v[c + 1, j], fence=(j == 0))
                if c >= 2:
                    sync.wait_ge(vs, p2[c - 2]["v_o"])
                else:
                    sync.wait_ge(pe, last1["t_rsum"])
                    sync.wait_ge(vs, last1["s_h2"][1])
                    dwait(sync, "ghn")
                dma(sync, par, hnb[:], hn_sp[c])
                rec["d_hn"] = (par, dn[par])

        def p2_mm(c, bg_ops=None):
            rec = p2[c]
            xqb = xqt_bufs[c % 2]
            hnb = hn_bufs[c % 2]

            for m in range(KD):
                @block.tensor
                def _(tensor):
                    if m == 0:
                        tensor.wait_ge(dsem_of[rec["d_ld"][0]],
                                       16 * rec["d_ld"][1])
                    if c == 0 and m % 4 == 0:
                        tensor.wait_ge(gw, 16 * G_WG[m // 4])
                    if m >= 2:
                        tensor.wait_ge(vs, rec[f"v_gr{m - 2}"])
                    elif c >= 1:
                        tensor.wait_ge(vs, p2[c - 1][f"v_gr{m + 14}"])
                    for k in range(KD):
                        mm = tensor.matmul(
                            pf[m % 2][:],
                            w8a[:, m * D + k * P:m * D + (k + 1) * P],
                            xqb[:, k * TC:(k + 1) * TC],
                            start=(k == 0), stop=(k == KD - 1))
                    mm.then_inc(pe, 1)
                n.t += 1
                rec[f"t_g{m}"] = n.t

                @block.vector
                def _(vector):
                    vector.wait_ge(pe, rec[f"t_g{m}"])
                    if m == 0:
                        if c >= 1:
                            vector.wait_ge(ss, p2[c - 1]["s_silu"])
                    V(vector.tensor_scalar(xq_tm[:, m * TC:(m + 1) * TC],
                                           pf[m % 2][:], sg_s[:, m:m + 1],
                                           None, ALU.mult))
                    rec[f"v_gr{m}"] = n.v
                    if bg_ops is not None and m % 2 == 1:
                        bg_ops[m // 2](vector)
            rec["t_g"] = rec[f"t_g{KD - 1}"]

        def p2_silu_o(c):
            rec = p2[c]
            hnb = hn_bufs[c % 2]

            @block.scalar
            def _(scalar):
                scalar.wait_ge(vs, rec[f"v_gr{KD - 1}"])
                if c >= 1:
                    scalar.wait_ge(vs, p2[c - 1]["v_o"])
                scalar.activation(f_b[:], xq_tm[:], ACT.Silu
                                  ).then_inc(ss, 1)
            n.s += 1
            rec["s_silu"] = n.s

            @block.vector
            def _(vector):
                vector.wait_ge(ss, rec["s_silu"])
                vector.wait_ge(dsem_of[rec["d_hn"][0]],
                               16 * rec["d_hn"][1])
                if c >= 1:
                    vector.wait_ge(pe, p2[c - 1]["t_wo"])
                vector.wait_ge(vs, n.v)
                V(vector.tensor_tensor(fm1[:], hnb[:], f_b[:], ALU.mult))
            rec["v_o"] = n.v

        def p2_tail(c, bg_ops=None):
            rec = p2[c]
            xb = x_bufs[c % 2]
            hb = h_bufs[c % 2]
            idx = 0
            for j in range(JT):
                for c4 in range(4):
                    @block.tensor
                    def _(tensor):
                        if idx == 0:
                            tensor.wait_ge(vs, rec["v_o"])
                        if c == 0 and j == 0:
                            tensor.wait_ge(gw2, 16 * G_WOT[c4])
                        if idx >= 2:
                            tensor.wait_ge(vs, rec[f"v_w{idx - 2}"])
                        elif c >= 1:
                            tensor.wait_ge(vs, p2[c - 1][f"v_w{idx + 6}"])
                        for k in range(KD):
                            mm = tensor.matmul(
                                ptr[idx % 2][:],
                                fm1[:, k * TC + j * P:k * TC + (j + 1) * P],
                                w8b[:, c4 * 8192 + k * 512:
                                    c4 * 8192 + (k + 1) * 512],
                                start=(k == 0), stop=(k == KD - 1))
                        mm.then_inc(pe, 1)
                    n.t += 1
                    rec[f"t_w{idx}"] = n.t

                    e1s = xn[:, (512 + 1024 * (idx % 2)):
                             (1024 + 1024 * (idx % 2))]
                    tss = xn[:, (1536 * (idx % 2)):
                             (512 + 1536 * (idx % 2))]

                    @block.vector
                    def _(vector):
                        vector.wait_ge(pe, rec[f"t_w{idx}"])
                        if idx >= 2:
                            vector.wait_ge(vs, rec[f"v_w{idx - 2}"])
                        elif c >= 1:
                            vector.wait_ge(vs, p2[c - 1][f"v_w{idx + 6}"])
                        if idx == 0:
                            if c >= 2:
                                vector.wait_ge(gx1,
                                               16 * p2[c - 2]["g_x1"])
                        V(vector.tensor_scalar(
                            e1s, ptr[idx % 2][:], 256.0, mgc[:, 0:1],
                            ALU.mult, ALU.add))
                        vector.wait_ge(vs, n.v)
                        V(vector.tensor_scalar(
                            tss, e1s, mgc[:, 0:1], 1.0 / 256.0,
                            ALU.subtract, ALU.mult))
                        vector.wait_ge(vs, n.v)
                        V(vector.tensor_tensor(
                            hb[:, j * D + c4 * 512:j * D + (c4 + 1) * 512],
                            tss,
                            xb[:, j * D + c4 * 512:j * D + (c4 + 1) * 512],
                            ALU.add))
                        rec[f"v_w{idx}"] = n.v
                        if bg_ops is not None:
                            bg_ops[idx](vector)
                    idx += 1
            rec["t_wo"] = rec[f"t_w{idx - 1}"]
            rec["v_x1"] = rec[f"v_w{idx - 1}"]

        def p2_stats_big(c):
            rec = p2[c]
            hb = h_bufs[c % 2]

            def mk(q):
                def f(vector):
                    j, kind = q // 4, q % 4
                    if q == 0 or kind == 3:
                        vector.wait_ge(vs, n.v)
                    x1j = hb[:, j * D:(j + 1) * D]
                    if kind == 0:
                        V(vector.reduce_sum(sumc[:, j:j + 1], x1j, AX))
                    elif kind == 1:
                        V(vector.reduce_max(tcol[:, j:j + 1], x1j, AX))
                    elif kind == 2:
                        V(vector.tensor_reduce(tcol[:, JT + j:JT + j + 1],
                                               x1j, AX, ALU.min))
                    else:
                        V(vector.scalar_tensor_tensor(
                            sqdump[:], x1j, 1.0, x1j,
                            ALU.mult, ALU.mult, accum_out=sqc[:, j:j + 1]))
                return f
            return [mk(q) for q in range(8)]

        def p2_stats_fin(c):
            rec = p2[c]
            hb = h_bufs[c % 2]

            @block.vector
            def _(vector):
                vector.wait_ge(vs, n.v)
                V(vector.tensor_scalar(mcol[:, 0:JT], sumc[:, 0:JT],
                                       1.0 / D, None, ALU.mult))
                V(vector.tensor_scalar(sqc[:, 0:JT], sqc[:, 0:JT],
                                       1.0 / D, None, ALU.mult))
                vector.wait_ge(vs, n.v)
                V(vector.tensor_tensor(tcol[:, 2 * JT:3 * JT], mcol[:, 0:JT],
                                       mcol[:, 0:JT], ALU.mult))
                vector.wait_ge(vs, n.v)
                V(vector.tensor_tensor(tcol[:, 2 * JT:3 * JT], sqc[:, 0:JT],
                                       tcol[:, 2 * JT:3 * JT], ALU.subtract))
            rec["v_var2"] = n.v

            @block.scalar
            def _(scalar):
                scalar.wait_ge(vs, rec["v_var2"])
                scalar.activation(tcol[:, 3 * JT:4 * JT],
                                  tcol[:, 2 * JT:3 * JT], ACT.Sqrt,
                                  bias=epsc[:, 0:1]).then_inc(ss, 1)
            n.s += 1
            rec["s_std2"] = n.s

            @block.vector
            def _(vector):
                vector.wait_ge(ss, rec["s_std2"])
                V(vector.reciprocal(rcol[:, 0:JT], tcol[:, 3 * JT:4 * JT]))
                vector.wait_ge(vs, n.v)
                V(vector.tensor_copy(stats_m[:, c * JT:(c + 1) * JT],
                                     mcol[:, 0:JT]))
                V(vector.tensor_copy(stats_r[:, c * JT:(c + 1) * JT],
                                     rcol[:, 0:JT]))
                V(vector.tensor_tensor(tcol[:, 0:JT], tcol[:, 0:JT],
                                       mcol[:, 0:JT], ALU.subtract))
                V(vector.tensor_tensor(tcol[:, JT:2 * JT], mcol[:, 0:JT],
                                       tcol[:, JT:2 * JT], ALU.subtract))
                vector.wait_ge(vs, n.v)
                V(vector.tensor_tensor(tcol[:, 0:JT], tcol[:, 0:JT],
                                       tcol[:, JT:2 * JT], ALU.max))
                vector.wait_ge(vs, n.v)
                V(vector.tensor_tensor(tcol[:, 0:JT], tcol[:, 0:JT],
                                       rcol[:, 0:JT], ALU.mult))
                vector.wait_ge(vs, n.v)
                V(vector.tensor_tensor(tcol[:, 0:1], tcol[:, 0:1],
                                       tcol[:, 1:JT], ALU.max))
                vector.wait_ge(vs, n.v)
                V(vector.tensor_tensor(runmax[:], runmax[:], tcol[:, 0:1],
                                       ALU.max))
            rec["v_stat"] = n.v

            @block.gpsimd
            def _(gpsimd):
                gpsimd.wait_ge(vs, rec["v_stat"])
                dma(gpsimd, "gx1", x1_sp[c], hb[:])
            rec["g_x1"] = dn["gx1"]

        for c in range(NRC):
            p2_load(c)
            p2_mm(c, p2_stats_big(c - 2) if c >= 2 else None)
            if c >= 2:
                p2_stats_fin(c - 2)
            if c >= 1:
                p2_tail(c - 1)
            p2_silu_o(c)
        p2_tail(NRC - 1, p2_stats_big(NRC - 2))
        p2_stats_fin(NRC - 2)

        @block.vector
        def _(vector):
            for f in p2_stats_big(NRC - 1):
                f(vector)
        p2_stats_fin(NRC - 1)

        @block.scalar
        def _(scalar):
            scalar.wait_ge(vs, p2[NRC - 1]["v_stat"])
            dma(scalar, "scc", cc_in[:, :], runmax[:])
            dma(scalar, "dsc", bias_s[:], bias_bc)
        B_CNT = dn["dsc"]

        @block.gpsimd
        def _(gpsimd):
            dwait(gpsimd, "scc")
            gpsimd.collective_compute(
                "AllReduce", ALU.max,
                replica_groups=[list(range(N_CORES))],
                ins=[cc_in.ap().opt()], outs=[cc_out.ap().opt()],
            ).then_inc(cs, 1)
            gpsimd.wait_ge(pe, p2[NRC - 1]["t_g"])
            dma(gpsimd, "gw", w8a[:], wft)
            gpsimd.wait_ge(cs, 1)
            dma(gpsimd, "gcc", rowg[0:1, 0:P],
                cc_out.ap().rearrange("p o -> o p"))
        G_P3W = dn["gw"]
        G_CC = dn["gcc"]

        @block.vector
        def _(vector):
            vector.wait_ge(gcc, 16 * G_CC)
            V(vector.reduce_max(grow[0:1, 0:1], rowg[0:1, 0:P], AX))
            vector.wait_ge(vs, n.v)
            V(vector.tensor_scalar(grow[0:1, 1:2], grow[0:1, 0:1],
                                   1.0 / 448.0, 1e-7, ALU.mult, ALU.max))
            vector.wait_ge(vs, n.v)
            V(vector.reciprocal(grow[0:1, 2:3], grow[0:1, 1:2]))
            vector.wait_ge(vs, n.v)
            V(vector.tensor_scalar(grow[0:1, 3:4], grow[0:1, 2:3],
                                   0.5, None, ALU.mult))
        V_GMAX = n.v

        @block.tensor
        def _(tensor):
            tensor.wait_ge(vs, V_GMAX)
            tensor.matmul(pbc[:, 0:1], onesrow[:], grow[0:1, 3:4],
                          start=True, stop=True).then_inc(pe, 1)
        n.t += 1
        T_BCG = n.t

        @block.vector
        def _(vector):
            vector.wait_ge(pe, T_BCG)
            V(vector.tensor_copy(invh[:], pbc[:, 0:1]))
            vector.wait_ge(vs, n.v)
            V(vector.tensor_scalar(stats_r2[:], stats_r[:], invh[:, 0:1],
                                   None, ALU.mult))
            vector.wait_ge(vs, n.v)
            V(vector.tensor_tensor(stats_b[:], stats_m[:], stats_r2[:],
                                   ALU.mult))
            vector.wait_ge(vs, n.v)
            V(vector.tensor_scalar(stats_b[:], stats_b[:], -1.0, None,
                                   ALU.mult))
        V_R2 = n.v

        p3 = {}
        last2 = p2[NRC - 1]

        def p3_head_a(c):
            rec = {}
            p3[c] = rec
            xb = x_bufs[c % 2]

            @block.sync
            def _(sync):
                if c >= 2:
                    sync.wait_ge(vs, p3[c - 2]["v_out"])
                elif c == 0:
                    sync.wait_ge(vs, p2[NRC - 2]["v_x1"])
                else:
                    sync.wait_ge(vs, p2[NRC - 1]["v_x1"])
                if c < 2:
                    dwait(sync, "gx1")
                par = dxp[c % 2]
                dma(sync, par, xb[:], x1_sp[c])
                rec["d_x1"] = (par, dn[par])

            @block.vector
            def _(vector):
                vector.wait_ge(dsem_of[rec["d_x1"][0]],
                               16 * rec["d_x1"][1])
                vector.wait_ge(vs, V_R2)
                if c >= 1:
                    vector.wait_ge(pe, p3[c - 1]["t_tq"])
                for j in range(JT):
                    V(vector.tensor_scalar(
                        zq[:, j * D:(j + 1) * D], xb[:, j * D:(j + 1) * D],
                        stats_r2[:, c * JT + j:c * JT + j + 1],
                        stats_b[:, c * JT + j:c * JT + j + 1],
                        ALU.mult, ALU.add))
            rec["v_q"] = n.v

        def p3_head_b(c):
            rec = p3[c]
            for p in range(8):
                @block.tensor
                def _(tensor):
                    if p == 0:
                        tensor.wait_ge(vs, rec["v_q"])
                        if c == 0:
                            tensor.wait_ge(gw, 16 * G_P3W)
                        else:
                            tensor.wait_ge(vs, p3[c - 1]["v_f6"])
                    if p == 1 and c >= 1:
                        tensor.wait_ge(vs, p3[c - 1]["v_f7"])
                    if p >= 2:
                        tensor.wait_ge(vs, rec[f"s_tq{p - 2}"][1])
                    bank = ptr[p % 2]
                    for kk in range(2):
                        k = 2 * p + kk
                        for j in range(JT):
                            mm = tensor.matmul(
                                bank[:, kk * TC + j * P:
                                     kk * TC + (j + 1) * P],
                                zq[:, j * D + k * P:j * D + (k + 1) * P],
                                idm_b[:], start=True, stop=True)
                    mm.then_inc(pe, 1)
                n.t += 1
                rec[f"t_tq{p}"] = n.t

                @block.vector
                def _(vector):
                    vector.wait_ge(pe, rec[f"t_tq{p}"])
                    if p == 0 and c >= 1:
                        vector.wait_ge(pe, p3[c - 1]["t_f7"])
                    V(vector.tensor_copy(zqt[:, 2 * p * TC:(2 * p + 2) * TC],
                                         ptr[p % 2][:]))
                rec[f"s_tq{p}"] = ("v", n.v)
            rec["t_tq"] = rec["t_tq7"]

        def p3_tail(c):
            rec = p3[c]
            xb = x_bufs[c % 2]
            hb = h_bufs[c % 2]
            idx = 0
            for j in range(JT):
                for c4 in range(4):
                    @block.tensor
                    def _(tensor):
                        if idx == 0:
                            tensor.wait_ge(vs, rec["s_tq7"][1])
                        if idx >= 2:
                            tensor.wait_ge(vs, rec[f"v_f{idx - 2}"])
                        for k in range(KD):
                            mm = tensor.matmul(
                                ptr[idx % 2][:],
                                zqt[:, k * TC + j * P:k * TC + (j + 1) * P],
                                w8a[:, c4 * 8192 + k * 512:
                                    c4 * 8192 + (k + 1) * 512],
                                start=(k == 0), stop=(k == KD - 1))
                        mm.then_inc(pe, 1)
                    n.t += 1
                    rec[f"t_f{idx}"] = n.t

                    @block.vector
                    def _(vector):
                        vector.wait_ge(pe, rec[f"t_f{idx}"])
                        vector.wait_ge(vs, n.v)
                        if idx == 0:
                            if c == 0:
                                vector.wait_ge(dsc, 16 * B_CNT)
                            if c >= 2:
                                vector.wait_ge(gout,
                                               16 * p3[c - 2]["g_out"])
                            else:
                                vector.wait_ge(gx1,
                                               16 * last2["g_x1"])
                        V(vector.scalar_tensor_tensor(
                            xn[:, 0:512], ptr[idx % 2][:], ws2_s[:, 0:1],
                            bias_s[:, c4 * 512:(c4 + 1) * 512],
                            ALU.mult, ALU.add))
                        vector.wait_ge(vs, n.v)
                        V(vector.tensor_tensor(
                            hb[:, j * D + c4 * 512:j * D + (c4 + 1) * 512],
                            xn[:, 0:512],
                            xb[:, j * D + c4 * 512:j * D + (c4 + 1) * 512],
                            ALU.add))
                    rec[f"v_f{idx}"] = n.v
                    idx += 1
            rec["t_f7"] = rec[f"t_f{idx - 1}"]
            rec["v_f7"] = rec[f"v_f{idx - 1}"]
            rec["v_out"] = rec[f"v_f{idx - 1}"]

            @block.gpsimd
            def _(gpsimd):
                gpsimd.wait_ge(vs, rec["v_out"])
                for j in range(JT):
                    dma(gpsimd, "gout", out_v[c, j],
                        hb[:, j * D:(j + 1) * D], fence=(j == 0))
            rec["g_out"] = dn["gout"]

        for c in range(NRC):
            p3_head_a(c)
            if c >= 1:
                p3_tail(c - 1)
            p3_head_b(c)
        p3_tail(NRC - 1)

        @block.sync
        def _(sync):
            for name in dn:
                dwait(sync, name)

    return nc


_NC_CACHE = {}


def _tile_w_m(w):
    return np.ascontiguousarray(
        w.reshape(KD, P, KD, P).transpose(1, 2, 0, 3).reshape(P, KD * D))


def _tile_w_c4(wt):
    return np.ascontiguousarray(
        wt.reshape(KD, P, 4, 512).transpose(1, 2, 0, 3).reshape(P, KD * D))


def _tile_s(s):
    return np.ascontiguousarray(s.reshape(KD, P).T)


def kernel(L0_attn_hidden_states, L0_attn_w_i, L0_attn_w_f, L0_attn_w_g,
           L0_attn_w_o, L0_attn_s_i, L0_attn_s_f, L0_attn_s_g, L0_attn_s_o,
           L0_attn_g_norm_w, L0_ffn_weight, L0_ffn_weight_scale,
           L0_ffn_bias):
    x = np.ascontiguousarray(
        np.asarray(L0_attn_hidden_states, dtype=np.float32)
    ).reshape(2 * 16384, D)

    wi_t = _tile_w_m(np.asarray(L0_attn_w_i, dtype=np.float32))
    wf_t = _tile_w_m(np.asarray(L0_attn_w_f, dtype=np.float32))
    wg_t = _tile_w_m(np.asarray(L0_attn_w_g, dtype=np.float32))
    wot_t = _tile_w_c4(np.ascontiguousarray(
        np.asarray(L0_attn_w_o, dtype=np.float32).T))
    wft_t = _tile_w_c4(np.ascontiguousarray(
        np.asarray(L0_ffn_weight, dtype=np.float32).T))
    si_t = _tile_s(np.asarray(L0_attn_s_i, dtype=np.float32))
    sf_t = _tile_s(np.asarray(L0_attn_s_f, dtype=np.float32))
    sg_t = _tile_s(np.asarray(L0_attn_s_g, dtype=np.float32))
    so_t = _tile_s(np.asarray(L0_attn_s_o, dtype=np.float32))
    gnw_t = _tile_s(np.asarray(L0_attn_g_norm_w, dtype=np.float32))
    bias_t = np.ascontiguousarray(np.broadcast_to(
        np.asarray(L0_ffn_bias, dtype=np.float32), (P, D)))
    ws_t = np.ascontiguousarray(np.broadcast_to(
        np.asarray(L0_ffn_weight_scale, dtype=np.float32).reshape(1, 1),
        (P, 1)))
    idm_t = np.eye(P, dtype=np.float32)

    in_maps = []
    for c in range(N_CORES):
        start = c * TOK
        if c % 4 == 0:
            warm = np.zeros((W, D), dtype=np.float32)
        else:
            warm = x[start - W:start]
        xw_c = np.ascontiguousarray(
            np.concatenate([warm, x[start:start + TOK]], axis=0))
        in_maps.append(dict(
            xw=xw_c, wi=wi_t, wf=wf_t, wg=wg_t, wot=wot_t, wft=wft_t,
            s_i=si_t, s_f=sf_t, s_g=sg_t, s_o=so_t, gnw=gnw_t,
            bias_bc=bias_t, ws_bc=ws_t, idm=idm_t,
        ))

    if "nc" not in _NC_CACHE:
        _NC_CACHE["nc"] = build()
    res = run_bass_kernel_spmd(_NC_CACHE["nc"], in_maps,
                               core_ids=list(range(N_CORES)))
    shards = [res.results[c]["out"] for c in range(N_CORES)]
    return np.concatenate(shards, axis=0).reshape(2, 16384, D)


# version 23
# speedup vs baseline: 1.0127x; 1.0127x over previous
import numpy as np
from contextlib import ExitStack

import concourse.bass as bass
import concourse.mybir as mybir
from concourse.bass_utils import run_bass_kernel_spmd

F32 = mybir.dt.float32
BF16 = mybir.dt.bfloat16
FP8 = mybir.dt.float8e4
AX = mybir.AxisListType.X
ALU = mybir.AluOpType
ACT = mybir.ActivationFunctionType

P = 128
D = 2048
KD = D // P
TC = 256
JT = TC // P
TOK = 4096
NRC = TOK // TC
W = 256
NCH = NRC + 1
MAGIC = 12582912.0
EPS = 1e-5
N_CORES = 8


class Ctr:
    def __init__(self):
        self.d = 0
        self.g = 0
        self.v = 0
        self.s = 0
        self.t = 0


def build():
    nc = bass.Bass("TRN2", target_bir_lowering=False, debug=False)

    xw = nc.dram_tensor("xw", [W + TOK, D], F32, kind="ExternalInput").ap()
    wi = nc.dram_tensor("wi", [P, KD * D], F32, kind="ExternalInput").ap()
    wf = nc.dram_tensor("wf", [P, KD * D], F32, kind="ExternalInput").ap()
    wg = nc.dram_tensor("wg", [P, KD * D], F32, kind="ExternalInput").ap()
    wot = nc.dram_tensor("wot", [P, KD * D], F32, kind="ExternalInput").ap()
    wft = nc.dram_tensor("wft", [P, KD * D], F32, kind="ExternalInput").ap()
    s_i = nc.dram_tensor("s_i", [P, KD], F32, kind="ExternalInput").ap()
    s_f = nc.dram_tensor("s_f", [P, KD], F32, kind="ExternalInput").ap()
    s_g = nc.dram_tensor("s_g", [P, KD], F32, kind="ExternalInput").ap()
    s_o = nc.dram_tensor("s_o", [P, KD], F32, kind="ExternalInput").ap()
    gnw = nc.dram_tensor("gnw", [P, KD], F32, kind="ExternalInput").ap()
    bias_bc = nc.dram_tensor("bias_bc", [P, D], F32,
                             kind="ExternalInput").ap()
    ws_bc = nc.dram_tensor("ws_bc", [P, 1], F32, kind="ExternalInput").ap()
    idm = nc.dram_tensor("idm", [P, P], F32, kind="ExternalInput").ap()

    out = nc.dram_tensor("out", [TOK, D], F32, kind="ExternalOutput").ap()
    out_v = out.rearrange("(c j p) d -> c j p d", j=JT, p=P)
    xw_v = xw.rearrange("(c j p) d -> c j p d", j=JT, p=P)

    xqt_sp = nc.dram_tensor("xqt_sp", [NRC, P, KD * TC], BF16).ap()
    hn_sp = nc.dram_tensor("hn_sp", [NRC, P, KD * TC], BF16).ap()
    x1_sp = nc.dram_tensor("x1_sp", [NRC, P, JT * D], F32).ap()
    cc_in = nc.dram_tensor("cc_in", [P, 1], F32)
    cc_out = nc.dram_tensor("cc_out", [P, 1], F32)

    n = Ctr()

    with ExitStack() as st:
        e = st.enter_context
        w8a = e(nc.sbuf_tensor("w8a", [P, KD * D], FP8))
        w8b = e(nc.sbuf_tensor("w8b", [P, KD * D], FP8))
        x_bufs = [e(nc.sbuf_tensor(f"x{i}", [P, JT * D], F32))
                  for i in range(2)]
        h_bufs = [e(nc.sbuf_tensor(f"h{i}", [P, KD * TC], F32))
                  for i in range(2)]
        xn = e(nc.sbuf_tensor("xn", [P, D], F32))
        xq_tm = e(nc.sbuf_tensor("xq_tm", [P, JT * D], BF16))
        xqt_b = [e(nc.sbuf_tensor(f"xqt{i}", [P, KD * TC], BF16))
                 for i in range(2)]
        f_b = e(nc.sbuf_tensor("f_b", [P, KD * TC], BF16))
        fm1 = e(nc.sbuf_tensor("fm1", [P, KD * TC], BF16))
        ih = e(nc.sbuf_tensor("ih", [P, KD * TC], BF16))
        state = e(nc.sbuf_tensor("state", [P, KD], F32))
        hn = e(nc.sbuf_tensor("hn", [P, KD * TC], BF16))
        rmsbc = e(nc.sbuf_tensor("rmsbc", [P, TC], F32))
        e1 = xn[:, 512:1024]
        zq = hn.bitcast(FP8)[:, 0:JT * D]
        zqt = hn.bitcast(FP8)[:, JT * D:2 * JT * D]
        stats_m = e(nc.sbuf_tensor("stats_m", [P, JT * NRC], F32))
        stats_r = e(nc.sbuf_tensor("stats_r", [P, JT * NRC], F32))
        stats_r2 = e(nc.sbuf_tensor("stats_r2", [P, JT * NRC], F32))
        stats_b = e(nc.sbuf_tensor("stats_b", [P, JT * NRC], F32))
        srow = e(nc.sbuf_tensor("srow", [1, TC], F32))
        rrow = e(nc.sbuf_tensor("rrow", [1, TC], F32))
        rcol = e(nc.sbuf_tensor("rcol", [P, JT], F32))
        mcol = e(nc.sbuf_tensor("mcol", [P, JT], F32))
        rrcol = e(nc.sbuf_tensor("rrcol", [P, JT], F32))
        tcol = e(nc.sbuf_tensor("tcol", [P, 4 * JT], F32))
        sumc = e(nc.sbuf_tensor("sumc", [P, JT], F32))
        sqc = e(nc.sbuf_tensor("sqc", [P, JT], F32))
        runmax = e(nc.sbuf_tensor("runmax", [P, 1], F32))
        rmax2 = e(nc.sbuf_tensor("rmax2", [P, 1], F32))
        rowg = e(nc.sbuf_tensor("rowg", [1, P], F32))
        grow = e(nc.sbuf_tensor("grow", [1, 4], F32))
        invh = e(nc.sbuf_tensor("invh", [P, 1], F32))
        mgc = e(nc.sbuf_tensor("mgc", [P, 1], F32))
        epsc = e(nc.sbuf_tensor("epsc", [P, 1], F32))
        onescol = e(nc.sbuf_tensor("onescol", [P, 1], BF16))
        onesrow = e(nc.sbuf_tensor("onesrow", [1, P], F32))
        idm_b = e(nc.sbuf_tensor("idm_b", [P, P], BF16))
        si_s = e(nc.sbuf_tensor("si_s", [P, KD], F32))
        sf_s = e(nc.sbuf_tensor("sf_s", [P, KD], F32))
        sg_s = e(nc.sbuf_tensor("sg_s", [P, KD], F32))
        so_s = e(nc.sbuf_tensor("so_s", [P, KD], F32))
        gnw_s = e(nc.sbuf_tensor("gnw_s", [P, KD], F32))
        nsf_s = e(nc.sbuf_tensor("nsf_s", [P, KD], F32))
        gso_s = e(nc.sbuf_tensor("gso_s", [P, KD], F32))
        wsb_s = e(nc.sbuf_tensor("wsb_s", [P, 1], F32))
        ws2_s = e(nc.sbuf_tensor("ws2_s", [P, 1], F32))
        bias_s = e(nc.sbuf_tensor("bias_s", [P, D], F32))
        sqdump = bias_s
        ptrA = e(nc.psum_tensor("ptrA", [P, 512], F32))
        ptrB = e(nc.psum_tensor("ptrB", [P, 512], F32))
        pfA = e(nc.psum_tensor("pfA", [P, TC], F32))
        pfB = e(nc.psum_tensor("pfB", [P, TC], F32))
        piA = e(nc.psum_tensor("piA", [P, 512], F32))
        piB = e(nc.psum_tensor("piB", [P, 512], F32))
        prow = e(nc.psum_tensor("prow", [1, TC], F32))
        pbc = e(nc.psum_tensor("pbc", [P, TC], F32))
        dsc = e(nc.semaphore("dsc"))
        dse = e(nc.semaphore("dse"))
        dso = e(nc.semaphore("dso"))
        gw = e(nc.semaphore("gw"))
        gxq = e(nc.semaphore("gxq"))
        ghn = e(nc.semaphore("ghn"))
        gx1 = e(nc.semaphore("gx1"))
        gout = e(nc.semaphore("gout"))
        gcc = e(nc.semaphore("gcc"))
        scc = e(nc.semaphore("scc"))
        gw2 = e(nc.semaphore("gw2"))
        gw3 = e(nc.semaphore("gw3"))
        vs = e(nc.semaphore("vs"))
        ss = e(nc.semaphore("ss"))
        pe = e(nc.semaphore("pe"))
        cs = e(nc.semaphore("cs"))
        block = e(nc.Block())

        dn = {"dsc": 0, "dse": 0, "dso": 0, "gw": 0, "gxq": 0,
              "ghn": 0, "gx1": 0, "gout": 0, "gcc": 0, "scc": 0,
              "gw2": 0, "gw3": 0}
        dsem_of = {"dsc": dsc, "dse": dse, "dso": dso, "gw": gw,
                   "gxq": gxq, "ghn": ghn, "gx1": gx1, "gout": gout,
                   "gcc": gcc, "scc": scc, "gw2": gw2, "gw3": gw3}
        dxp = ["dse", "dso"]

        def dma(eng, name, out_ap, in_ap, fence=True, **kw):
            if fence and dn[name] > 0:
                eng.wait_ge(dsem_of[name], 16 * dn[name])
            eng.dma_start(out=out_ap, in_=in_ap, **kw).then_inc(
                dsem_of[name], 16)
            dn[name] += 1
            return dn[name]

        def dwait(eng, name, cnt=None):
            eng.wait_ge(dsem_of[name], 16 * (dn[name] if cnt is None
                                             else cnt))

        def V(inst):
            inst.then_inc(vs, 1)
            n.v += 1
            return n.v

        ptr = [ptrA, ptrB]
        pf = [pfA, pfB]
        pi = [piA, piB]

        @block.sync
        def _(sync):
            dma(sync, "dsc", si_s[:], s_i)
            dma(sync, "dsc", sf_s[:], s_f, fence=False)
            dma(sync, "dsc", sg_s[:], s_g, fence=False)
            dma(sync, "dsc", so_s[:], s_o, fence=False)
            dma(sync, "dsc", gnw_s[:], gnw, fence=False)
            dma(sync, "dsc", wsb_s[:], ws_bc, fence=False)
            for j in range(JT):
                dma(sync, "dse",
                    x_bufs[0][:, j * D:(j + 1) * D], xw_v[0, j],
                    fence=(j == 0))
        x0_tgt = ("dse", dn["dse"])

        G_WF = [0] * 4
        G_WI = [0] * 4

        @block.gpsimd
        def _(gpsimd):
            dma(gpsimd, "gw", idm_b[:], idm)
        G_IDM = dn["gw"]

        @block.gpsimd
        def _(gpsimd):
            for g4 in range(4):
                for mm4 in range(4):
                    m = g4 * 4 + mm4
                    dma(gpsimd, "gw", w8b[:, m * D:(m + 1) * D],
                        wf[:, m * D:(m + 1) * D], fence=(mm4 == 0))
                G_WF[g4] = dn["gw"]
            for g4 in range(4):
                for mm4 in range(4):
                    m = g4 * 4 + mm4
                    dma(gpsimd, "gw", w8a[:, m * D:(m + 1) * D],
                        wi[:, m * D:(m + 1) * D], fence=(mm4 == 0))
                G_WI[g4] = dn["gw"]
        G_SETUP = dn["gw"]

        @block.vector
        def _(vector):
            V(vector.memset(mgc[:], MAGIC))
            V(vector.memset(epsc[:], EPS))
            V(vector.memset(onescol[:], 1.0))
            V(vector.memset(onesrow[:], 1.0))
            V(vector.memset(runmax[:], 0.0))
            V(vector.memset(state[:], 0.0))
            dwait(vector, "dsc")
            V(vector.tensor_scalar(nsf_s[:], sf_s[:], -1.0, None, ALU.mult))
            V(vector.tensor_tensor(gso_s[:], gnw_s[:], so_s[:], ALU.mult))
            V(vector.tensor_scalar(ws2_s[:], wsb_s[:], 2.0, None, ALU.mult))
        V_SETUP = n.v

        p1 = {}

        def p1_norm_pre(ci):
            rec = p1.setdefault(ci, {})
            if ci + 1 < NCH:
                @block.sync
                def _(sync):
                    if ci >= 1:
                        sync.wait_ge(vs, p1[ci - 1]["v_norm"])
                    for j in range(JT):
                        dma(sync, dxp[(ci + 1) % 2],
                            x_bufs[(ci + 1) % 2][:, j * D:(j + 1) * D],
                            xw_v[ci + 1, j], fence=(j == 0))
                rec["d_x_next"] = (dxp[(ci + 1) % 2],
                                   dn[dxp[(ci + 1) % 2]])
            else:
                rec["d_x_next"] = None

            xb = x_bufs[ci % 2]
            d_x_ready = p1[ci - 1]["d_x_next"] if ci >= 1 else x0_tgt

            @block.scalar
            def _(scalar):
                if ci == 0:
                    scalar.wait_ge(vs, V_SETUP)
                scalar.wait_ge(dsem_of[d_x_ready[0]], 16 * d_x_ready[1])
                for j in range(JT):
                    if j == 1:
                        scalar.wait_ge(ss, n.s + 1)
                    scalar.activation(
                        sqdump[:],
                        xb[:, j * D:(j + 1) * D], ACT.Square,
                        accum_out=sqc[:, j:j + 1],
                    ).then_inc(ss, 1)
            n.s += JT
            rec["s_sq"] = n.s

            @block.vector
            def _(vector):
                vector.wait_ge(vs, n.v)
                vector.wait_ge(dsem_of[d_x_ready[0]], 16 * d_x_ready[1])
                for j in range(JT):
                    V(vector.reduce_sum(sumc[:, j:j + 1],
                                        xb[:, j * D:(j + 1) * D], AX))
                vector.wait_ge(ss, rec["s_sq"])
                vector.wait_ge(vs, n.v)
                V(vector.tensor_scalar(mcol[:, 0:JT], sumc[:, 0:JT],
                                       1.0 / D, None, ALU.mult))
                V(vector.tensor_scalar(tcol[:, JT:2 * JT], sqc[:, 0:JT],
                                       1.0 / D, None, ALU.mult))
                vector.wait_ge(vs, n.v)
                V(vector.tensor_tensor(tcol[:, 0:JT], mcol[:, 0:JT],
                                       mcol[:, 0:JT], ALU.mult))
                vector.wait_ge(vs, n.v)
                V(vector.tensor_tensor(tcol[:, 2 * JT:3 * JT],
                                       tcol[:, JT:2 * JT],
                                       tcol[:, 0:JT], ALU.subtract))
            rec["v_var"] = n.v

        def p1_norm_sqrt(ci):
            rec = p1[ci]

            @block.scalar
            def _(scalar):
                scalar.wait_ge(vs, rec["v_var"])
                scalar.activation(tcol[:, 3 * JT:4 * JT],
                                  tcol[:, 2 * JT:3 * JT], ACT.Sqrt,
                                  bias=epsc[:, 0:1]).then_inc(ss, 1)
            n.s += 1
            rec["s_std"] = n.s

        def p1_norm_fin_ops(ci):
            rec = p1[ci]
            xb = x_bufs[ci % 2]

            def op0(vector):
                vector.wait_ge(ss, rec["s_std"])
                V(vector.reciprocal(rcol[:, 0:JT], tcol[:, 3 * JT:4 * JT]))

            def op1(vector):
                vector.wait_ge(vs, n.v)
                V(vector.tensor_scalar(rrcol[:, 0:JT], rcol[:, 0:JT],
                                       256.0, None, ALU.mult))

            def op2(vector):
                vector.wait_ge(vs, n.v)
                V(vector.tensor_tensor(tcol[:, 0:JT], mcol[:, 0:JT],
                                       rrcol[:, 0:JT], ALU.mult))
                vector.wait_ge(vs, n.v)
                V(vector.tensor_scalar(tcol[:, JT:2 * JT], tcol[:, 0:JT],
                                       -1.0, mgc[:, 0:1],
                                       ALU.mult, ALU.add))

            def mk_big(j, second):
                def f(vector):
                    vector.wait_ge(vs, n.v)
                    if not second:
                        if j == 0 and ci >= 1:
                            vector.wait_ge(pe, p1[ci - 1]["t_tr"])
                        V(vector.tensor_scalar(
                            xn[:], xb[:, j * D:(j + 1) * D],
                            rrcol[:, j:j + 1], tcol[:, JT + j:JT + j + 1],
                            ALU.mult, ALU.add))
                    else:
                        V(vector.tensor_scalar(
                            xq_tm[:, j * D:(j + 1) * D], xn[:],
                            mgc[:, 0:1], 1.0 / 256.0,
                            ALU.subtract, ALU.mult))
                        if j == JT - 1:
                            rec["v_norm"] = n.v
                return f

            return [op0, op1, op2, mk_big(0, False), mk_big(0, True),
                    mk_big(1, False), mk_big(1, True)]

        def p1_norm_full(ci):
            p1_norm_pre(ci)
            p1_norm_sqrt(ci)
            ops = p1_norm_fin_ops(ci)

            @block.vector
            def _(vector):
                for f in ops:
                    f(vector)

        tr4 = [ptrA, ptrB, piA, piB]

        def p1_tr(ci, mid_hook=None):
            rec = p1.setdefault(ci, {})
            for p in range(8):
                @block.tensor
                def _(tensor):
                    if p == 0:
                        if ci == 0:
                            tensor.wait_ge(gw, 16 * G_IDM)
                        tensor.wait_ge(vs, rec["v_norm"])
                    if p in (2, 3) and ci >= 2:
                        tensor.wait_ge(vs, p1[ci - 2][f"v_i{12 + p}"])
                    if p >= 4:
                        tensor.wait_ge(vs, rec[f"v_tr{p - 4}"][1])
                    bank = tr4[p % 4]
                    for kk in range(2):
                        k = 2 * p + kk
                        for j in range(JT):
                            mm = tensor.matmul(
                                bank[:, kk * TC + j * P:
                                     kk * TC + (j + 1) * P],
                                xq_tm[:, j * D + k * P:j * D + (k + 1) * P],
                                idm_b[:], start=True, stop=True)
                    mm.then_inc(pe, 1)
                n.t += 1
                rec[f"t_tr{p}"] = n.t

                @block.vector
                def _(vector):
                    vector.wait_ge(pe, rec[f"t_tr{p}"])
                    if p == 0 and ci >= 2:
                        vector.wait_ge(pe, p1[ci - 2]["t_i_last"])
                        if "g_xqt" in p1[ci - 2]:
                            vector.wait_ge(gxq,
                                           16 * p1[ci - 2]["g_xqt"])
                    V(vector.tensor_copy(
                        xqt_b[ci % 2][:, 2 * p * TC:(2 * p + 2) * TC],
                        tr4[p % 4][:]))
                rec[f"v_tr{p}"] = ("v", n.v)
                if p == 1 and mid_hook is not None:
                    mid_hook()
            rec["t_tr"] = rec["t_tr7"]
            rec["v_tr"] = rec["v_tr7"]

            if ci >= 1:
                @block.gpsimd
                def _(gpsimd):
                    gpsimd.wait_ge(vs, rec["v_tr"][1])
                    dma(gpsimd, "gxq", xqt_sp[ci - 1], xqt_b[ci % 2][:])
                rec["g_xqt"] = dn["gxq"]

        def p1_head_b(ci, hn_ops=None, norm_chunk=None):
            rec = p1[ci]
            hb = h_bufs[ci % 2]
            xqt = xqt_b[ci % 2]
            tr_rec = p1[min(ci + 1, NCH - 1)]
            norm_ops = (p1_norm_fin_ops(norm_chunk)
                        if norm_chunk is not None else None)
            for m in range(KD):
                @block.tensor
                def _(tensor):
                    if m == 0:
                        tensor.wait_ge(vs, rec["v_tr"][1])
                    if ci == 0 and m % 4 == 0:
                        tensor.wait_ge(gw, 16 * G_WF[m // 4])
                    if m >= 2:
                        tensor.wait_ge(ss, rec[f"s_f{m - 2}"])
                    elif ci >= 1:
                        tensor.wait_ge(ss, p1[ci - 1][f"s_f{m + 14}"])
                    for k in range(KD):
                        mm = tensor.matmul(
                            pf[m % 2][:],
                            w8b[:, m * D + k * P:m * D + (k + 1) * P],
                            xqt[:, k * TC:(k + 1) * TC],
                            start=(k == 0), stop=(k == KD - 1))
                    mm.then_inc(pe, 1)
                n.t += 1
                rec[f"t_f{m}"] = n.t

                @block.scalar
                def _(scalar):
                    scalar.wait_ge(pe, rec[f"t_f{m}"])
                    if m == 0 and ci >= 1:
                        scalar.wait_ge(vs, p1[ci - 1]["v_scan"])
                    scalar.activation(f_b[:, m * TC:(m + 1) * TC],
                                      pf[m % 2][:], ACT.Sigmoid,
                                      scale=sf_s[:, m:m + 1]).then_inc(ss, 1)
                    scalar.activation(fm1[:, m * TC:(m + 1) * TC],
                                      pf[m % 2][:], ACT.Sigmoid,
                                      scale=nsf_s[:, m:m + 1]).then_inc(ss, 1)
                n.s += 2
                rec[f"s_f{m}"] = n.s

                @block.tensor
                def _(tensor):
                    if ci == 0 and m % 4 == 0:
                        tensor.wait_ge(gw, 16 * G_WI[m // 4])
                    if m >= 2:
                        tensor.wait_ge(vs, rec[f"v_i{m - 2}"])
                    elif ci >= 1:
                        tensor.wait_ge(vs, p1[ci - 1][f"v_i{m + 14}"])
                    if m < 2:
                        tensor.wait_ge(vs, tr_rec[f"v_tr{6 + m}"][1])
                    for k in range(KD):
                        mm = tensor.matmul(
                            pi[m % 2][:, 0:TC],
                            w8a[:, m * D + k * P:m * D + (k + 1) * P],
                            xqt[:, k * TC:(k + 1) * TC],
                            start=(k == 0), stop=(k == KD - 1))
                    mm.then_inc(pe, 1)
                n.t += 1
                rec[f"t_i{m}"] = n.t

                @block.vector
                def _(vector):
                    vector.wait_ge(pe, rec[f"t_i{m}"])
                    vector.wait_ge(ss, rec[f"s_f{m}"])
                    if m == 0 and ci >= 2:
                        vector.wait_ge(pe, p1[ci - 1]["t_rsum"])
                    if m == 0 and ci >= 1:
                        vector.wait_ge(vs, p1[ci - 1]["v_save"])
                    V(vector.scalar_tensor_tensor(
                        ih[:, m * TC:(m + 1) * TC], pi[m % 2][:, 0:TC],
                        si_s[:, m:m + 1], fm1[:, m * TC:(m + 1) * TC],
                        ALU.mult, ALU.mult))
                    vector.wait_ge(vs, n.v)
                    V(vector.tensor_tensor_scan(
                        hb[:, m * TC:(m + 1) * TC],
                        f_b[:, m * TC:(m + 1) * TC],
                        ih[:, m * TC:(m + 1) * TC],
                        state[:, m:m + 1], ALU.mult, ALU.add))
                    if ci >= 1:
                        vector.wait_ge(vs, n.v)
                        V(vector.tensor_tensor(
                            ih[:, m * TC:(m + 1) * TC],
                            hb[:, m * TC:(m + 1) * TC],
                            hb[:, m * TC:(m + 1) * TC], ALU.mult))
                    rec[f"v_h2k{m}"] = n.v
                    rec[f"v_i{m}"] = n.v
                    if hn_ops is not None and m < 8:
                        hn_ops[m](vector)
                    if norm_ops is not None and 8 <= m <= 14:
                        norm_ops[m - 8](vector)
                if norm_chunk is not None and m == 6:
                    p1_norm_sqrt(norm_chunk)
            rec["t_i_last"] = rec[f"t_i{KD - 1}"]
            rec["v_scan"] = n.v
            rec["s_h2"] = ("v", n.v)

            @block.vector
            def _(vector):
                vector.wait_ge(vs, n.v)
                V(vector.tensor_copy(state[:], hb[:, TC - 1::TC]))
            rec["v_save"] = n.v

        def p1_tail_pe(ci):
            rec = p1[ci]

            @block.tensor
            def _(tensor):
                if ci >= 2:
                    tensor.wait_ge(ss, p1[ci - 1]["s_rms"])
                for k in range(KD):
                    tensor.wait_ge(vs, rec[f"v_h2k{k}"])
                    mm = tensor.matmul(prow[:], onescol[:],
                                       ih[:, k * TC:(k + 1) * TC],
                                       start=(k == 0), stop=(k == KD - 1))
                mm.then_inc(pe, 1)
            n.t += 1
            rec["t_rsum"] = n.t

        def p1_tail_s(ci):
            rec = p1[ci]

            @block.scalar
            def _(scalar):
                scalar.wait_ge(pe, rec["t_rsum"])
                scalar.activation(srow[:], prow[:], ACT.Sqrt,
                                  bias=epsc[0:1, 0:1], scale=1.0 / D
                                  ).then_inc(ss, 1)
            n.s += 1
            rec["s_rms"] = n.s

        def p1_tail_v(ci):
            rec = p1[ci]

            @block.vector
            def _(vector):
                vector.wait_ge(ss, rec["s_rms"])
                if ci >= 2:
                    vector.wait_ge(pe, p1[ci - 1]["t_bc"])
                V(vector.reciprocal(rrow[:], srow[:]))
            rec["v_rr"] = n.v

        def p1_bc_pe(ci):
            rec = p1[ci]

            @block.tensor
            def _(tensor):
                tensor.wait_ge(vs, rec["v_rr"])
                if ci >= 2:
                    tensor.wait_ge(vs, p1[ci - 1]["v_bc"])
                tensor.matmul(pbc[:], onesrow[:], rrow[:],
                              start=True, stop=True).then_inc(pe, 1)
            n.t += 1
            rec["t_bc"] = n.t

        def p1_bc_v(ci):
            rec = p1[ci]

            @block.vector
            def _(vector):
                vector.wait_ge(pe, rec["t_bc"])
                V(vector.tensor_copy(rmsbc[:], pbc[:]))
            rec["v_bc"] = n.v

        def p1_hn_ops(ci):
            rec = p1[ci]
            hb = h_bufs[ci % 2]

            def mk(q):
                def f(vector):
                    if q == 0:
                        if ci >= 2:
                            vector.wait_ge(ghn, 16 * p1[ci - 1]["g_hn"])
                        vector.wait_ge(vs, n.v)
                    for k in (2 * q, 2 * q + 1):
                        V(vector.scalar_tensor_tensor(
                            hn[:, k * TC:(k + 1) * TC],
                            hb[:, k * TC:(k + 1) * TC],
                            gso_s[:, k:k + 1], rmsbc[:],
                            ALU.mult, ALU.mult))
                    if q == 7:
                        rec["v_hn"] = n.v
                return f
            return [mk(q) for q in range(8)]

        def p1_hn_sp(ci):
            rec = p1[ci]

            @block.gpsimd
            def _(gpsimd):
                gpsimd.wait_ge(vs, rec["v_hn"])
                dma(gpsimd, "ghn", hn_sp[ci - 1], hn[:])
            rec["g_hn"] = dn["ghn"]

        p1_norm_full(0)
        for ci in range(NCH):
            if ci >= 3:
                p1_tail_pe(ci - 2)
                p1_tail_s(ci - 2)
                p1_tr(ci, mid_hook=lambda c=ci: p1_tail_v(c - 2))
            else:
                p1_tr(ci)
            if ci >= 3:
                p1_bc_pe(ci - 2)
                p1_bc_v(ci - 2)
            if ci == 0:
                p1_norm_full(1)
            elif ci + 1 < NCH:
                p1_norm_pre(ci + 1)
            if ci >= 1:
                p1_head_b(
                    ci - 1,
                    hn_ops=p1_hn_ops(ci - 2) if ci >= 3 else None,
                    norm_chunk=ci + 1 if 2 <= ci + 1 < NCH else None)
            if ci >= 3:
                p1_hn_sp(ci - 2)

        p1_tail_pe(NCH - 2)
        p1_tail_s(NCH - 2)
        p1_tail_v(NCH - 2)
        p1_bc_pe(NCH - 2)
        p1_bc_v(NCH - 2)
        p1_head_b(NCH - 1, hn_ops=p1_hn_ops(NCH - 2))
        p1_hn_sp(NCH - 2)

        G_WG = [0] * 4
        G_WOT = [0] * 4
        lastp1 = p1[NCH - 1]

        @block.gpsimd
        def _(gpsimd):
            for g4 in range(4):
                gpsimd.wait_ge(pe, lastp1[f"t_f{4 * g4 + 3}"])
                dma(gpsimd, "gw2", w8b[:, g4 * 8192:(g4 + 1) * 8192],
                    wot[:, g4 * 8192:(g4 + 1) * 8192])
                G_WOT[g4] = dn["gw2"]
                gpsimd.wait_ge(pe, lastp1[f"t_i{4 * g4 + 3}"])
                ws = "gw" if g4 % 2 == 0 else "gw3"
                for mm4 in range(4):
                    m = 4 * g4 + mm4
                    dma(gpsimd, ws, w8a[:, m * D:(m + 1) * D],
                        wg[:, m * D:(m + 1) * D], fence=(mm4 == 0))
                G_WG[g4] = (dsem_of[ws], dn[ws])

        p1_tail_pe(NCH - 1)
        p1_tail_s(NCH - 1)
        p1_tail_v(NCH - 1)
        p1_bc_pe(NCH - 1)
        p1_bc_v(NCH - 1)
        last_hn = p1_hn_ops(NCH - 1)

        @block.vector
        def _(vector):
            for f in last_hn:
                f(vector)
        p1_hn_sp(NCH - 1)

        p2 = {}
        xqt_bufs = xqt_b
        hn_bufs = [hn, ih]
        last1 = p1[NCH - 1]

        def p2_load(c):
            rec = {}
            p2[c] = rec
            xqb = xqt_bufs[c % 2]
            hnb = hn_bufs[c % 2]
            xb = x_bufs[c % 2]

            @block.sync
            def _(sync):
                par = dxp[c % 2]
                if c >= 2:
                    sync.wait_ge(pe, p2[c - 2]["t_g"])
                else:
                    sync.wait_ge(pe, last1["t_i_last"])
                    dwait(sync, "gxq")
                dma(sync, par, xqb[:], xqt_sp[c])
                rec["d_ld"] = (par, dn[par])
                if c >= 2:
                    sync.wait_ge(vs, p2[c - 2]["v_x1"])
                for j in range(JT):
                    dma(sync, par, xb[:, j * D:(j + 1) * D],
                        xw_v[c + 1, j], fence=(j == 0))
                if c >= 2:
                    sync.wait_ge(vs, p2[c - 2]["v_o"])
                else:
                    sync.wait_ge(pe, last1["t_rsum"])
                    sync.wait_ge(vs, last1["s_h2"][1])
                    dwait(sync, "ghn")
                dma(sync, par, hnb[:], hn_sp[c])
                rec["d_hn"] = (par, dn[par])

        def p2_mm(c, bg_ops=None):
            rec = p2[c]
            xqb = xqt_bufs[c % 2]
            hnb = hn_bufs[c % 2]

            for m in range(KD):
                @block.tensor
                def _(tensor):
                    if m == 0:
                        tensor.wait_ge(dsem_of[rec["d_ld"][0]],
                                       16 * rec["d_ld"][1])
                    if c == 0 and m % 4 == 0:
                        tensor.wait_ge(G_WG[m // 4][0],
                                       16 * G_WG[m // 4][1])
                    if m >= 2:
                        tensor.wait_ge(vs, rec[f"v_gr{m - 2}"])
                    elif c >= 1:
                        tensor.wait_ge(vs, p2[c - 1][f"v_gr{m + 14}"])
                    for k in range(KD):
                        mm = tensor.matmul(
                            pf[m % 2][:],
                            w8a[:, m * D + k * P:m * D + (k + 1) * P],
                            xqb[:, k * TC:(k + 1) * TC],
                            start=(k == 0), stop=(k == KD - 1))
                    mm.then_inc(pe, 1)
                n.t += 1
                rec[f"t_g{m}"] = n.t

                @block.vector
                def _(vector):
                    vector.wait_ge(pe, rec[f"t_g{m}"])
                    if m == 0:
                        if c >= 1:
                            vector.wait_ge(ss, p2[c - 1]["s_silu"])
                    V(vector.tensor_scalar(xq_tm[:, m * TC:(m + 1) * TC],
                                           pf[m % 2][:], sg_s[:, m:m + 1],
                                           None, ALU.mult))
                    rec[f"v_gr{m}"] = n.v
                    if bg_ops is not None and m % 2 == 1:
                        bg_ops[m // 2](vector)
            rec["t_g"] = rec[f"t_g{KD - 1}"]

        def p2_silu_o(c):
            rec = p2[c]
            hnb = hn_bufs[c % 2]

            @block.scalar
            def _(scalar):
                scalar.wait_ge(vs, rec[f"v_gr{KD - 1}"])
                if c >= 1:
                    scalar.wait_ge(vs, p2[c - 1]["v_o"])
                scalar.activation(f_b[:], xq_tm[:], ACT.Silu
                                  ).then_inc(ss, 1)
            n.s += 1
            rec["s_silu"] = n.s

            @block.vector
            def _(vector):
                vector.wait_ge(ss, rec["s_silu"])
                vector.wait_ge(dsem_of[rec["d_hn"][0]],
                               16 * rec["d_hn"][1])
                if c >= 1:
                    vector.wait_ge(pe, p2[c - 1]["t_wo"])
                vector.wait_ge(vs, n.v)
                V(vector.tensor_tensor(fm1[:], hnb[:], f_b[:], ALU.mult))
            rec["v_o"] = n.v

        def p2_tail(c, bg_ops=None):
            rec = p2[c]
            xb = x_bufs[c % 2]
            hb = h_bufs[c % 2]
            idx = 0
            for j in range(JT):
                for c4 in range(4):
                    @block.tensor
                    def _(tensor):
                        if idx == 0:
                            tensor.wait_ge(vs, rec["v_o"])
                        if c == 0 and j == 0:
                            tensor.wait_ge(gw2, 16 * G_WOT[c4])
                        if idx >= 2:
                            tensor.wait_ge(vs, rec[f"v_w{idx - 2}"])
                        elif c >= 1:
                            tensor.wait_ge(vs, p2[c - 1][f"v_w{idx + 6}"])
                        for k in range(KD):
                            mm = tensor.matmul(
                                ptr[idx % 2][:],
                                fm1[:, k * TC + j * P:k * TC + (j + 1) * P],
                                w8b[:, c4 * 8192 + k * 512:
                                    c4 * 8192 + (k + 1) * 512],
                                start=(k == 0), stop=(k == KD - 1))
                        mm.then_inc(pe, 1)
                    n.t += 1
                    rec[f"t_w{idx}"] = n.t

                    e1s = xn[:, (512 + 1024 * (idx % 2)):
                             (1024 + 1024 * (idx % 2))]
                    tss = xn[:, (1536 * (idx % 2)):
                             (512 + 1536 * (idx % 2))]

                    @block.vector
                    def _(vector):
                        vector.wait_ge(pe, rec[f"t_w{idx}"])
                        if idx >= 2:
                            vector.wait_ge(vs, rec[f"v_w{idx - 2}"])
                        elif c >= 1:
                            vector.wait_ge(vs, p2[c - 1][f"v_w{idx + 6}"])
                        if idx == 0:
                            if c >= 2:
                                vector.wait_ge(gx1,
                                               16 * p2[c - 2]["g_x1"])
                        V(vector.tensor_scalar(
                            e1s, ptr[idx % 2][:], 256.0, mgc[:, 0:1],
                            ALU.mult, ALU.add))
                        vector.wait_ge(vs, n.v)
                        V(vector.tensor_scalar(
                            tss, e1s, mgc[:, 0:1], 1.0 / 256.0,
                            ALU.subtract, ALU.mult))
                        vector.wait_ge(vs, n.v)
                        V(vector.tensor_tensor(
                            hb[:, j * D + c4 * 512:j * D + (c4 + 1) * 512],
                            tss,
                            xb[:, j * D + c4 * 512:j * D + (c4 + 1) * 512],
                            ALU.add))
                        rec[f"v_w{idx}"] = n.v
                        if bg_ops is not None:
                            bg_ops[idx](vector)
                    idx += 1
            rec["t_wo"] = rec[f"t_w{idx - 1}"]
            rec["v_x1"] = rec[f"v_w{idx - 1}"]

        def p2_stats_big(c):
            rec = p2[c]
            hb = h_bufs[c % 2]

            def mk(q):
                def f(vector):
                    j, kind = q // 4, q % 4
                    if q == 0 or kind == 3:
                        vector.wait_ge(vs, n.v)
                    x1j = hb[:, j * D:(j + 1) * D]
                    if kind == 0:
                        V(vector.reduce_sum(sumc[:, j:j + 1], x1j, AX))
                    elif kind == 1:
                        V(vector.reduce_max(tcol[:, j:j + 1], x1j, AX))
                    elif kind == 2:
                        V(vector.tensor_reduce(tcol[:, JT + j:JT + j + 1],
                                               x1j, AX, ALU.min))
                    else:
                        V(vector.scalar_tensor_tensor(
                            sqdump[:], x1j, 1.0, x1j,
                            ALU.mult, ALU.mult, accum_out=sqc[:, j:j + 1]))
                return f
            return [mk(q) for q in range(8)]

        def p2_stats_fin(c):
            rec = p2[c]
            hb = h_bufs[c % 2]

            @block.vector
            def _(vector):
                vector.wait_ge(vs, n.v)
                V(vector.tensor_scalar(mcol[:, 0:JT], sumc[:, 0:JT],
                                       1.0 / D, None, ALU.mult))
                V(vector.tensor_scalar(sqc[:, 0:JT], sqc[:, 0:JT],
                                       1.0 / D, None, ALU.mult))
                vector.wait_ge(vs, n.v)
                V(vector.tensor_tensor(tcol[:, 2 * JT:3 * JT], mcol[:, 0:JT],
                                       mcol[:, 0:JT], ALU.mult))
                vector.wait_ge(vs, n.v)
                V(vector.tensor_tensor(tcol[:, 2 * JT:3 * JT], sqc[:, 0:JT],
                                       tcol[:, 2 * JT:3 * JT], ALU.subtract))
            rec["v_var2"] = n.v

            @block.scalar
            def _(scalar):
                scalar.wait_ge(vs, rec["v_var2"])
                scalar.activation(tcol[:, 3 * JT:4 * JT],
                                  tcol[:, 2 * JT:3 * JT], ACT.Sqrt,
                                  bias=epsc[:, 0:1]).then_inc(ss, 1)
            n.s += 1
            rec["s_std2"] = n.s

            @block.vector
            def _(vector):
                vector.wait_ge(ss, rec["s_std2"])
                V(vector.reciprocal(rcol[:, 0:JT], tcol[:, 3 * JT:4 * JT]))
                vector.wait_ge(vs, n.v)
                V(vector.tensor_copy(stats_m[:, c * JT:(c + 1) * JT],
                                     mcol[:, 0:JT]))
                V(vector.tensor_copy(stats_r[:, c * JT:(c + 1) * JT],
                                     rcol[:, 0:JT]))
                V(vector.tensor_tensor(tcol[:, 0:JT], tcol[:, 0:JT],
                                       mcol[:, 0:JT], ALU.subtract))
                V(vector.tensor_tensor(tcol[:, JT:2 * JT], mcol[:, 0:JT],
                                       tcol[:, JT:2 * JT], ALU.subtract))
                vector.wait_ge(vs, n.v)
                V(vector.tensor_tensor(tcol[:, 0:JT], tcol[:, 0:JT],
                                       tcol[:, JT:2 * JT], ALU.max))
                vector.wait_ge(vs, n.v)
                V(vector.tensor_tensor(tcol[:, 0:JT], tcol[:, 0:JT],
                                       rcol[:, 0:JT], ALU.mult))
                vector.wait_ge(vs, n.v)
                V(vector.tensor_tensor(tcol[:, 0:1], tcol[:, 0:1],
                                       tcol[:, 1:JT], ALU.max))
                vector.wait_ge(vs, n.v)
                V(vector.tensor_tensor(runmax[:], runmax[:], tcol[:, 0:1],
                                       ALU.max))
            rec["v_stat"] = n.v

            @block.gpsimd
            def _(gpsimd):
                gpsimd.wait_ge(vs, rec["v_stat"])
                dma(gpsimd, "gx1", x1_sp[c], hb[:])
            rec["g_x1"] = dn["gx1"]

        for c in range(NRC):
            p2_load(c)
            p2_mm(c, p2_stats_big(c - 2) if c >= 2 else None)
            if c >= 2:
                p2_stats_fin(c - 2)
            if c >= 1:
                p2_tail(c - 1)
            p2_silu_o(c)
        p2_tail(NRC - 1, p2_stats_big(NRC - 2))
        p2_stats_fin(NRC - 2)

        @block.vector
        def _(vector):
            for f in p2_stats_big(NRC - 1):
                f(vector)
        p2_stats_fin(NRC - 1)

        @block.scalar
        def _(scalar):
            scalar.wait_ge(vs, p2[NRC - 1]["v_stat"])
            dma(scalar, "scc", cc_in[:, :], runmax[:])
            dma(scalar, "dsc", bias_s[:], bias_bc)
        B_CNT = dn["dsc"]

        @block.gpsimd
        def _(gpsimd):
            dwait(gpsimd, "scc")
            gpsimd.collective_compute(
                "AllReduce", ALU.max,
                replica_groups=[list(range(N_CORES))],
                ins=[cc_in.ap().opt()], outs=[cc_out.ap().opt()],
            ).then_inc(cs, 1)
            gpsimd.wait_ge(pe, p2[NRC - 1]["t_g"])
            dma(gpsimd, "gw", w8a[:], wft)
            gpsimd.wait_ge(cs, 1)
            dma(gpsimd, "gcc", rowg[0:1, 0:P],
                cc_out.ap().rearrange("p o -> o p"))
        G_P3W = dn["gw"]
        G_CC = dn["gcc"]

        @block.vector
        def _(vector):
            vector.wait_ge(gcc, 16 * G_CC)
            V(vector.reduce_max(grow[0:1, 0:1], rowg[0:1, 0:P], AX))
            vector.wait_ge(vs, n.v)
            V(vector.tensor_scalar(grow[0:1, 1:2], grow[0:1, 0:1],
                                   1.0 / 448.0, 1e-7, ALU.mult, ALU.max))
            vector.wait_ge(vs, n.v)
            V(vector.reciprocal(grow[0:1, 2:3], grow[0:1, 1:2]))
            vector.wait_ge(vs, n.v)
            V(vector.tensor_scalar(grow[0:1, 3:4], grow[0:1, 2:3],
                                   0.5, None, ALU.mult))
        V_GMAX = n.v

        @block.tensor
        def _(tensor):
            tensor.wait_ge(vs, V_GMAX)
            tensor.matmul(pbc[:, 0:1], onesrow[:], grow[0:1, 3:4],
                          start=True, stop=True).then_inc(pe, 1)
        n.t += 1
        T_BCG = n.t

        @block.vector
        def _(vector):
            vector.wait_ge(pe, T_BCG)
            V(vector.tensor_copy(invh[:], pbc[:, 0:1]))
            vector.wait_ge(vs, n.v)
            V(vector.tensor_scalar(stats_r2[:], stats_r[:], invh[:, 0:1],
                                   None, ALU.mult))
            vector.wait_ge(vs, n.v)
            V(vector.tensor_tensor(stats_b[:], stats_m[:], stats_r2[:],
                                   ALU.mult))
            vector.wait_ge(vs, n.v)
            V(vector.tensor_scalar(stats_b[:], stats_b[:], -1.0, None,
                                   ALU.mult))
        V_R2 = n.v

        p3 = {}
        last2 = p2[NRC - 1]

        def p3_head_a(c):
            rec = {}
            p3[c] = rec
            xb = x_bufs[c % 2]

            @block.sync
            def _(sync):
                if c >= 2:
                    sync.wait_ge(vs, p3[c - 2]["v_out"])
                elif c == 0:
                    sync.wait_ge(vs, p2[NRC - 2]["v_x1"])
                else:
                    sync.wait_ge(vs, p2[NRC - 1]["v_x1"])
                if c < 2:
                    dwait(sync, "gx1")
                par = dxp[c % 2]
                dma(sync, par, xb[:], x1_sp[c])
                rec["d_x1"] = (par, dn[par])

            @block.vector
            def _(vector):
                vector.wait_ge(dsem_of[rec["d_x1"][0]],
                               16 * rec["d_x1"][1])
                vector.wait_ge(vs, V_R2)
                if c >= 1:
                    vector.wait_ge(pe, p3[c - 1]["t_tq"])
                for j in range(JT):
                    V(vector.tensor_scalar(
                        zq[:, j * D:(j + 1) * D], xb[:, j * D:(j + 1) * D],
                        stats_r2[:, c * JT + j:c * JT + j + 1],
                        stats_b[:, c * JT + j:c * JT + j + 1],
                        ALU.mult, ALU.add))
            rec["v_q"] = n.v

        def p3_head_b(c):
            rec = p3[c]
            for p in range(8):
                @block.tensor
                def _(tensor):
                    if p == 0:
                        tensor.wait_ge(vs, rec["v_q"])
                        if c == 0:
                            tensor.wait_ge(gw, 16 * G_P3W)
                        else:
                            tensor.wait_ge(vs, p3[c - 1]["v_f6"])
                    if p == 1 and c >= 1:
                        tensor.wait_ge(vs, p3[c - 1]["v_f7"])
                    if p >= 2:
                        tensor.wait_ge(vs, rec[f"s_tq{p - 2}"][1])
                    bank = ptr[p % 2]
                    for kk in range(2):
                        k = 2 * p + kk
                        for j in range(JT):
                            mm = tensor.matmul(
                                bank[:, kk * TC + j * P:
                                     kk * TC + (j + 1) * P],
                                zq[:, j * D + k * P:j * D + (k + 1) * P],
                                idm_b[:], start=True, stop=True)
                    mm.then_inc(pe, 1)
                n.t += 1
                rec[f"t_tq{p}"] = n.t

                @block.vector
                def _(vector):
                    vector.wait_ge(pe, rec[f"t_tq{p}"])
                    if p == 0 and c >= 1:
                        vector.wait_ge(pe, p3[c - 1]["t_f7"])
                    V(vector.tensor_copy(zqt[:, 2 * p * TC:(2 * p + 2) * TC],
                                         ptr[p % 2][:]))
                rec[f"s_tq{p}"] = ("v", n.v)
            rec["t_tq"] = rec["t_tq7"]

        def p3_tail(c):
            rec = p3[c]
            xb = x_bufs[c % 2]
            hb = h_bufs[c % 2]
            idx = 0
            for j in range(JT):
                for c4 in range(4):
                    @block.tensor
                    def _(tensor):
                        if idx == 0:
                            tensor.wait_ge(vs, rec["s_tq7"][1])
                        if idx >= 2:
                            tensor.wait_ge(vs, rec[f"v_f{idx - 2}"])
                        for k in range(KD):
                            mm = tensor.matmul(
                                ptr[idx % 2][:],
                                zqt[:, k * TC + j * P:k * TC + (j + 1) * P],
                                w8a[:, c4 * 8192 + k * 512:
                                    c4 * 8192 + (k + 1) * 512],
                                start=(k == 0), stop=(k == KD - 1))
                        mm.then_inc(pe, 1)
                    n.t += 1
                    rec[f"t_f{idx}"] = n.t

                    @block.vector
                    def _(vector):
                        vector.wait_ge(pe, rec[f"t_f{idx}"])
                        vector.wait_ge(vs, n.v)
                        if idx == 0:
                            if c == 0:
                                vector.wait_ge(dsc, 16 * B_CNT)
                            if c >= 2:
                                vector.wait_ge(gout,
                                               16 * p3[c - 2]["g_out"])
                            else:
                                vector.wait_ge(gx1,
                                               16 * last2["g_x1"])
                        V(vector.scalar_tensor_tensor(
                            xn[:, 0:512], ptr[idx % 2][:], ws2_s[:, 0:1],
                            bias_s[:, c4 * 512:(c4 + 1) * 512],
                            ALU.mult, ALU.add))
                        vector.wait_ge(vs, n.v)
                        V(vector.tensor_tensor(
                            hb[:, j * D + c4 * 512:j * D + (c4 + 1) * 512],
                            xn[:, 0:512],
                            xb[:, j * D + c4 * 512:j * D + (c4 + 1) * 512],
                            ALU.add))
                    rec[f"v_f{idx}"] = n.v
                    idx += 1
            rec["t_f7"] = rec[f"t_f{idx - 1}"]
            rec["v_f7"] = rec[f"v_f{idx - 1}"]
            rec["v_out"] = rec[f"v_f{idx - 1}"]

            @block.gpsimd
            def _(gpsimd):
                gpsimd.wait_ge(vs, rec["v_out"])
                for j in range(JT):
                    dma(gpsimd, "gout", out_v[c, j],
                        hb[:, j * D:(j + 1) * D], fence=(j == 0))
            rec["g_out"] = dn["gout"]

        for c in range(NRC):
            p3_head_a(c)
            if c >= 1:
                p3_tail(c - 1)
            p3_head_b(c)
        p3_tail(NRC - 1)

        @block.sync
        def _(sync):
            for name in dn:
                dwait(sync, name)

    return nc


_NC_CACHE = {}


def _tile_w_m(w):
    return np.ascontiguousarray(
        w.reshape(KD, P, KD, P).transpose(1, 2, 0, 3).reshape(P, KD * D))


def _tile_w_c4(wt):
    return np.ascontiguousarray(
        wt.reshape(KD, P, 4, 512).transpose(1, 2, 0, 3).reshape(P, KD * D))


def _tile_s(s):
    return np.ascontiguousarray(s.reshape(KD, P).T)


def kernel(L0_attn_hidden_states, L0_attn_w_i, L0_attn_w_f, L0_attn_w_g,
           L0_attn_w_o, L0_attn_s_i, L0_attn_s_f, L0_attn_s_g, L0_attn_s_o,
           L0_attn_g_norm_w, L0_ffn_weight, L0_ffn_weight_scale,
           L0_ffn_bias):
    x = np.ascontiguousarray(
        np.asarray(L0_attn_hidden_states, dtype=np.float32)
    ).reshape(2 * 16384, D)

    wi_t = _tile_w_m(np.asarray(L0_attn_w_i, dtype=np.float32))
    wf_t = _tile_w_m(np.asarray(L0_attn_w_f, dtype=np.float32))
    wg_t = _tile_w_m(np.asarray(L0_attn_w_g, dtype=np.float32))
    wot_t = _tile_w_c4(np.ascontiguousarray(
        np.asarray(L0_attn_w_o, dtype=np.float32).T))
    wft_t = _tile_w_c4(np.ascontiguousarray(
        np.asarray(L0_ffn_weight, dtype=np.float32).T))
    si_t = _tile_s(np.asarray(L0_attn_s_i, dtype=np.float32))
    sf_t = _tile_s(np.asarray(L0_attn_s_f, dtype=np.float32))
    sg_t = _tile_s(np.asarray(L0_attn_s_g, dtype=np.float32))
    so_t = _tile_s(np.asarray(L0_attn_s_o, dtype=np.float32))
    gnw_t = _tile_s(np.asarray(L0_attn_g_norm_w, dtype=np.float32))
    bias_t = np.ascontiguousarray(np.broadcast_to(
        np.asarray(L0_ffn_bias, dtype=np.float32), (P, D)))
    ws_t = np.ascontiguousarray(np.broadcast_to(
        np.asarray(L0_ffn_weight_scale, dtype=np.float32).reshape(1, 1),
        (P, 1)))
    idm_t = np.eye(P, dtype=np.float32)

    in_maps = []
    for c in range(N_CORES):
        start = c * TOK
        if c % 4 == 0:
            warm = np.zeros((W, D), dtype=np.float32)
        else:
            warm = x[start - W:start]
        xw_c = np.ascontiguousarray(
            np.concatenate([warm, x[start:start + TOK]], axis=0))
        in_maps.append(dict(
            xw=xw_c, wi=wi_t, wf=wf_t, wg=wg_t, wot=wot_t, wft=wft_t,
            s_i=si_t, s_f=sf_t, s_g=sg_t, s_o=so_t, gnw=gnw_t,
            bias_bc=bias_t, ws_bc=ws_t, idm=idm_t,
        ))

    if "nc" not in _NC_CACHE:
        _NC_CACHE["nc"] = build()
    res = run_bass_kernel_spmd(_NC_CACHE["nc"], in_maps,
                               core_ids=list(range(N_CORES)))
    shards = [res.results[c]["out"] for c in range(N_CORES)]
    return np.concatenate(shards, axis=0).reshape(2, 16384, D)
